# version 05
# baseline (speedup 1.0000x reference)
import numpy as np
import ml_dtypes
from contextlib import ExitStack

import concourse.bass as bass
import concourse.mybir as mybir
from concourse.bass_utils import run_bass_kernel_spmd

F32 = mybir.dt.float32
BF16 = mybir.dt.bfloat16
AF = mybir.ActivationFunctionType
ALU = mybir.AluOpType
AX = mybir.AxisListType
BF = ml_dtypes.bfloat16

B, S, D = 2, 4096, 4096
NKV = 8
E, R = 8, 16
NCORE = 8
T = 1024
DC = 32
SCALE = float(128) ** -0.5
NEG = -1.0e30
EPS = 1e-5


def _stage_inputs(inputs):
    x = np.asarray(inputs["L0_attn_hidden_states"], dtype=np.float32)
    Wq = np.asarray(inputs["L0_attn_Wq"], dtype=np.float32)
    Wk = np.asarray(inputs["L0_attn_Wk"], dtype=np.float32)
    Wv = np.asarray(inputs["L0_attn_Wv"], dtype=np.float32)
    Wo = np.asarray(inputs["L0_attn_Wo"], dtype=np.float32)
    Wb = np.asarray(inputs["L0_ffn_W_base"], dtype=np.float32)
    bb = np.asarray(inputs["L0_ffn_b_base"], dtype=np.float32)
    Wr = np.asarray(inputs["L0_ffn_W_router"], dtype=np.float32)
    br = np.asarray(inputs["L0_ffn_b_router"], dtype=np.float32)
    lA = np.asarray(inputs["L0_ffn_lora_A"], dtype=np.float32)
    lB = np.asarray(inputs["L0_ffn_lora_B"], dtype=np.float32)

    def qg_major(W_din_dout):
        w = W_din_dout.reshape(DC, 128, DC, 128)
        w = w.transpose(2, 0, 1, 3)
        w = w.reshape(16, 2, DC, 128, 128).transpose(0, 2, 3, 1, 4)
        return np.ascontiguousarray(w.reshape(16, DC, 128, 256))

    kvec = np.ascontiguousarray(np.broadcast_to(
        np.tile(np.arange(1, 9, dtype=np.float32), 8)[None, :], (128, 64)))

    shared = {
        "wq": qg_major(Wq),
        "wo": qg_major(Wo),
        "wb": qg_major(np.ascontiguousarray(Wb.T)),
        "wk": np.ascontiguousarray(Wk.reshape(DC, 128, 1024)),
        "wv": np.ascontiguousarray(Wv.reshape(DC, 128, 1024)),
        "wr": np.ascontiguousarray(Wr.T.reshape(DC, 128, E)),
        "la": np.ascontiguousarray(lA.transpose(1, 0, 2).reshape(DC, 128, E * R)),
        "lb": np.ascontiguousarray(lB.reshape(E * R, DC, 128).transpose(1, 0, 2)),
        "bbs": np.ascontiguousarray(bb.reshape(1, D)),
        "brs": np.ascontiguousarray(br.reshape(1, E)),
        "idm": np.eye(128, dtype=np.float32).astype(BF),
        "onescol": np.ones((128, 1), dtype=BF),
        "ones1f": np.ones((1, 128), dtype=np.float32),
        "ones1b": np.ones((1, 128), dtype=BF),
        "onesrow": np.ones((1, 512), dtype=BF),
        "e8m": np.kron(np.eye(E, dtype=np.float32),
                       np.ones((1, R), dtype=np.float32)).astype(BF),
        "kvecf": kvec,
    }

    xfm = [np.ascontiguousarray(x[bb_].T).reshape(DC, 128, S) for bb_ in range(B)]

    in_maps, own = [], []
    for c in range(NCORE):
        b_, i = c // 4, c % 4
        rows = np.concatenate([np.arange((4 * s + i) * 128, (4 * s + i + 1) * 128)
                               for s in range(8)])
        own.append((b_, rows))
        xo = np.ascontiguousarray(x[b_][rows].T).reshape(DC, 128, T)
        mk = np.zeros((DC, 128, 128), dtype=np.float32)
        for kc in range(DC):
            g = 4 * (kc // 4) + i
            if kc > g:
                mk[kc, :, :] = NEG
            elif kc == g:
                kk = np.arange(128)[:, None]
                qq = np.arange(128)[None, :]
                mk[kc] = np.where(kk <= qq, 0.0, NEG)
        m = dict(shared)
        m["xfm"] = xfm[b_]
        m["xown"] = xo
        m["maskstk"] = np.ascontiguousarray(mk.astype(BF))
        in_maps.append(m)
    return in_maps, own


def build_program(stop_after="all"):
    nc = bass.Bass("TRN2", target_bir_lowering=False)

    def din(name, shape, dt=F32):
        return nc.dram_tensor(name, shape, dt, kind="ExternalInput").ap()

    xfm = din("xfm", [DC, 128, S])
    xown = din("xown", [DC, 128, T])
    wq = din("wq", [16, DC, 128, 256])
    wo = din("wo", [16, DC, 128, 256])
    wb = din("wb", [16, DC, 128, 256])
    wk = din("wk", [DC, 128, 1024])
    wv = din("wv", [DC, 128, 1024])
    wr = din("wr", [DC, 128, E])
    la = din("la", [DC, 128, E * R])
    lb = din("lb", [DC, E * R, 128])
    bbs = din("bbs", [1, D])
    brs = din("brs", [1, E])
    idm = din("idm", [128, 128], BF16)
    onescol = din("onescol", [128, 1], BF16)
    ones1f = din("ones1f", [1, 128])
    ones1b = din("ones1b", [1, 128], BF16)
    onesrow = din("onesrow", [1, 512], BF16)
    e8m = din("e8m", [E, E * R], BF16)
    maskstk = din("maskstk", [DC, 128, 128], BF16)
    kvecf = din("kvecf", [128, 64])

    qdr = nc.dram_tensor("qdr", [D, T], BF16, kind="Internal").ap()
    kdr = nc.dram_tensor("kdr", [1024, S], BF16, kind="Internal").ap()
    audr = nc.dram_tensor("audr", [D, T], BF16, kind="Internal").ap()
    x1dr = nc.dram_tensor("x1dr", [DC, 128, T], BF16, kind="Internal").ap()
    x2dr = nc.dram_tensor("x2dr", [DC, 128, T], BF16, kind="Internal").ap()
    out = nc.dram_tensor("out", [D, T], F32, kind="ExternalOutput").ap()

    uid = [0]

    def _nm(p):
        uid[0] += 1
        return f"{p}{uid[0]}"

    dbg = {}
    if stop_after in ("q", "kv"):
        dbg["dq"] = nc.dram_tensor("dq", [D, T], BF16, kind="ExternalOutput").ap()
    if stop_after == "kv":
        dbg["dk"] = nc.dram_tensor("dk", [1024, S], BF16,
                                   kind="ExternalOutput").ap()
        dbg["dv"] = nc.dram_tensor("dv", [128, DC * 1024], BF16,
                                   kind="ExternalOutput").ap()
    if stop_after == "attn":
        dbg["dau"] = nc.dram_tensor("dau", [D, T], BF16,
                                    kind="ExternalOutput").ap()
    if stop_after == "x1":
        dbg["dx1"] = nc.dram_tensor("dx1", [128, DC * 1024], BF16,
                                    kind="ExternalOutput").ap()
    if stop_after in ("w", "g"):
        dbg["dz"] = nc.dram_tensor("dz", [128, 64], F32,
                                   kind="ExternalOutput").ap()
        dbg["dw"] = nc.dram_tensor("dw", [128, 64], F32,
                                   kind="ExternalOutput").ap()
    if stop_after == "g":
        dbg["dg"] = nc.dram_tensor("dg", [128, T], BF16,
                                   kind="ExternalOutput").ap()
    if stop_after == "x2":
        dbg["dx2"] = nc.dram_tensor("dx2", [DC, 128, T], BF16,
                                    kind="ExternalOutput").ap()
        dbg["dm2"] = nc.dram_tensor("dm2", [128, T], F32,
                                    kind="ExternalOutput").ap()
        dbg["dr2"] = nc.dram_tensor("dr2", [128, T], F32,
                                    kind="ExternalOutput").ap()

    with ExitStack() as top:
        c_id = top.enter_context(nc.sbuf_tensor("c_id", [128, 128], BF16))
        c_oc = top.enter_context(nc.sbuf_tensor("c_oc", [128, 1], BF16))
        c_1f = top.enter_context(nc.sbuf_tensor("c_1f", [1, 128], F32))
        c_1b = top.enter_context(nc.sbuf_tensor("c_1b", [1, 128], BF16))
        c_or = top.enter_context(nc.sbuf_tensor("c_or", [1, 512], BF16))
        c_e8 = top.enter_context(nc.sbuf_tensor("c_e8", [E, E * R], BF16))
        c_kv = top.enter_context(nc.sbuf_tensor("c_kv", [128, 64], F32))
        wr_b = top.enter_context(nc.sbuf_tensor("wr_b", [128, DC * E], BF16))
        bb_b = top.enter_context(nc.sbuf_tensor("bb_b", [1, D], BF16))
        br_b = top.enter_context(nc.sbuf_tensor("br_b", [1, E], BF16))
        m1 = top.enter_context(nc.sbuf_tensor("m1", [128, T], F32))
        r1 = top.enter_context(nc.sbuf_tensor("r1", [128, T], F32))
        eps11 = top.enter_context(nc.sbuf_tensor("eps11", [1, 1], F32))

        sems = {}
        for nm in ["ld", "st", "mm", "ac", "ev", "po",
                   "ssc", "spr", "scn", "kA", "kB", "kC",
                   "qA", "qB", "qC", "qD", "pA",
                   "rA", "rB", "rC", "rD", "sA"]:
            sems[nm] = top.enter_context(nc.semaphore(nm))
        n = {k: 0 for k in sems}

        def I(instr, nm, amt=1):
            instr.then_inc(sems[nm], amt)
            n[nm] += amt

        def W(eng, nm, val):
            if val > 0:
                eng.wait_ge(sems[nm], val)

        with ExitStack() as ph:
            wr_f = ph.enter_context(nc.sbuf_tensor("wr_f", [128, DC * E], F32))
            bb_f = ph.enter_context(nc.sbuf_tensor("bb_f", [1, D], F32))
            br_f = ph.enter_context(nc.sbuf_tensor("br_f", [1, E], F32))
            with nc.Block() as blk:
                @blk.sync
                def _(sync):
                    I(sync.dma_start(c_id[:], idm[:]), "ld", 16)
                    I(sync.dma_start(c_oc[:], onescol[:]), "ld", 16)
                    I(sync.dma_start(c_1f[:], ones1f[:]), "ld", 16)
                    I(sync.dma_start(c_1b[:], ones1b[:]), "ld", 16)
                    I(sync.dma_start(c_or[:], onesrow[:]), "ld", 16)
                    I(sync.dma_start(c_e8[:], e8m[:]), "ld", 16)
                    I(sync.dma_start(c_kv[:], kvecf[:]), "ld", 16)
                    I(sync.dma_start(
                        wr_f[:].rearrange("p (c e) -> p c e", c=DC),
                        wr[:].rearrange("c p e -> p c e")), "ld", 16)
                    I(sync.dma_start(bb_f[:], bbs[:]), "ld", 16)
                    I(sync.dma_start(br_f[:], brs[:]), "ld", 16)

                @blk.vector
                def _(vector):
                    vector.memset(eps11[:], EPS)
                    W(vector, "ld", n["ld"])
                    vector.tensor_copy(wr_b[:], wr_f[:])
                    vector.tensor_copy(bb_b[:], bb_f[:])
                    vector.tensor_copy(br_b[:], br_f[:])

        def load_cast_resident(dst_sb, src_dram, nchunks=DC, width=1024):
            with ExitStack() as ph:
                stg = [ph.enter_context(
                    nc.sbuf_tensor(_nm("lcr"), [128, width], F32)) for _ in range(2)]
                b = dict(n)
                with nc.Block() as blk:
                    @blk.sync
                    def _(sync):
                        for dc in range(nchunks):
                            if dc >= 2:
                                W(sync, "ac", b["ac"] + dc - 1)
                            I(sync.dma_start(stg[dc % 2][:], src_dram[dc]), "ld", 16)

                    @blk.scalar
                    def _(scalar):
                        for dc in range(nchunks):
                            W(scalar, "ld", b["ld"] + 16 * (dc + 1))
                            I(scalar.copy(
                                dst_sb[:, dc * width:(dc + 1) * width],
                                stg[dc % 2][:]), "ac")


        def _dump_and_stop(pairs, sbuf_pairs=()):
            bq = dict(n)
            with nc.Block() as blk:
                @blk.sync
                def _(sync):
                    W(sync, "st", n["st"])
                    k = 0
                    for dst, srcap in pairs:
                        I(sync.dma_start(dst[:], srcap[:]), "st", 16)
                        k += 1
                    for dst, sb in sbuf_pairs:
                        I(sync.dma_start(dst[:], sb[:]), "st", 16)
                        k += 1
                    W(sync, "st", bq["st"] + 16 * k)

        with ExitStack() as ph:
            xo_b = ph.enter_context(nc.sbuf_tensor("xo_b", [128, DC * 1024], BF16))
            load_cast_resident(xo_b, xown)
            w_stg = ph.enter_context(nc.sbuf_tensor(_nm("ws"), [128, DC * 128], F32))
            w_bb = [ph.enter_context(nc.sbuf_tensor(_nm("wb"), [128, DC * 256], BF16))
                    for _ in range(2)]
            q_stg = [ph.enter_context(nc.sbuf_tensor(_nm("qs"), [128, 1024], BF16))
                     for _ in range(2)]
            pq = [ph.enter_context(nc.psum_tensor(_nm("pq"), [128, 1024], F32))
                  for _ in range(4)]
            b = dict(n)
            with nc.Block() as blk:
                @blk.sync
                def _(sync):
                    for qg in range(16):
                        for hf in range(2):
                            W(sync, "ev", b["ev"] + 2 * qg + hf)
                            I(sync.dma_start(
                                w_stg[:].rearrange("p (c f) -> p c f", c=16),
                                wq[qg, 16 * hf:16 * (hf + 1)]
                                .rearrange("c p f -> p c f")), "ld", 16)

                @blk.vector
                def _(vector):
                    for qg in range(16):
                        for hf in range(2):
                            W(vector, "ld", b["ld"] + 16 * (2 * qg + hf + 1))
                            if qg >= 2 and hf == 0:
                                W(vector, "mm", b["mm"] + qg - 1)
                            I(vector.tensor_copy(
                                w_bb[qg % 2][:, hf * 4096:(hf + 1) * 4096],
                                w_stg[:]), "ev")

                @blk.tensor
                def _(tensor):
                    for qg in range(16):
                        W(tensor, "ev", b["ev"] + 2 * (qg + 1))
                        if qg >= 2:
                            W(tensor, "ac", b["ac"] + 2 * (qg - 1))
                        par = 2 * (qg % 2)
                        ins = None
                        for dc in range(DC):
                            for o in range(2):
                                for h in range(2):
                                    ins = tensor.matmul(
                                        pq[par + o][:, h * 512:(h + 1) * 512],
                                        w_bb[qg % 2][:, dc * 256 + o * 128:dc * 256 + (o + 1) * 128],
                                        xo_b[:, dc * 1024 + h * 512:dc * 1024 + (h + 1) * 512],
                                        start=(dc == 0), stop=(dc == DC - 1))
                        I(ins, "mm")

                @blk.scalar
                def _(scalar):
                    for qg in range(16):
                        W(scalar, "mm", b["mm"] + qg + 1)
                        par = 2 * (qg % 2)
                        for o in range(2):
                            W(scalar, "st", b["st"] + 16 * (2 * qg + o - 1))
                            I(scalar.activation(q_stg[o][:], pq[par + o][:],
                                                AF.Copy, scale=SCALE), "ac")

                @blk.gpsimd
                def _(gpsimd):
                    for qg in range(16):
                        for o in range(2):
                            W(gpsimd, "ac", b["ac"] + 2 * qg + o + 1)
                            I(gpsimd.dma_start(
                                qdr[(2 * qg + o) * 128:(2 * qg + o + 1) * 128, :],
                                q_stg[o][:]), "st", 16)

        if stop_after == "q":
            _dump_and_stop([(dbg["dq"], qdr)])
            return nc

        with ExitStack() as ph:
            wk_bb = ph.enter_context(nc.sbuf_tensor("wk_bb", [128, DC * 1024], BF16))
            load_cast_resident(wk_bb, wk)
            xs = [ph.enter_context(nc.sbuf_tensor(_nm("x2s"), [128, 512], F32))
                  for _ in range(2)]
            xb = [ph.enter_context(nc.sbuf_tensor(_nm("x2b"), [128, 512], BF16))
                  for _ in range(2)]
            k_stg = [ph.enter_context(nc.sbuf_tensor(_nm("ks"), [128, 512], BF16))
                     for _ in range(2)]
            pk = [ph.enter_context(nc.psum_tensor(_nm("pk"), [128, 512], F32))
                  for _ in range(8)]
            b = dict(n)
            with nc.Block() as blk:
                @blk.sync
                def _(sync):
                    for ts in range(8):
                        for dc in range(DC):
                            idx = ts * DC + dc
                            if idx >= 2:
                                W(sync, "ac", b["ac"] + idx - 1)
                            I(sync.dma_start(
                                xs[idx % 2][:],
                                xfm[dc][:, ts * 512:(ts + 1) * 512]), "ld", 16)

                @blk.scalar
                def _(scalar):
                    for ts in range(8):
                        for dc in range(DC):
                            idx = ts * DC + dc
                            W(scalar, "ld", b["ld"] + 16 * (idx + 1))
                            if idx >= 2:
                                W(scalar, "mm", b["mm"] + idx - 1)
                            I(scalar.copy(xb[idx % 2][:], xs[idx % 2][:]), "ac")

                @blk.tensor
                def _(tensor):
                    for ts in range(8):
                        for dc in range(DC):
                            idx = ts * DC + dc
                            W(tensor, "ac", b["ac"] + idx + 1)
                            if dc == 0 and ts >= 1:
                                W(tensor, "ev", b["ev"] + 8 * ts)
                            ins = None
                            for dkc in range(8):
                                ins = tensor.matmul(
                                    pk[dkc][:],
                                    wk_bb[:, dc * 1024 + dkc * 128:dc * 1024 + (dkc + 1) * 128],
                                    xb[idx % 2][:],
                                    start=(dc == 0), stop=(dc == DC - 1))
                            I(ins, "mm")

                @blk.vector
                def _(vector):
                    for ts in range(8):
                        W(vector, "mm", b["mm"] + (ts + 1) * DC)
                        for dkc in range(8):
                            j = ts * 8 + dkc
                            if j >= 2:
                                W(vector, "st", b["st"] + 16 * (j - 1))
                            I(vector.tensor_copy(k_stg[j % 2][:], pk[dkc][:]), "ev")

                @blk.gpsimd
                def _(gpsimd):
                    for ts in range(8):
                        for dkc in range(8):
                            j = ts * 8 + dkc
                            W(gpsimd, "ev", b["ev"] + j + 1)
                            I(gpsimd.dma_start(
                                kdr[dkc * 128:(dkc + 1) * 128, ts * 512:(ts + 1) * 512],
                                k_stg[j % 2][:]), "st", 16)

        ph35 = top.enter_context(ExitStack())
        v_sb = ph35.enter_context(nc.sbuf_tensor("v_sb", [128, DC * 1024], BF16))
        msk = ph35.enter_context(nc.sbuf_tensor("msk", [128, DC * 128], BF16))
        with ExitStack() as ph:
            wv_bb = ph.enter_context(nc.sbuf_tensor("wv_bb", [128, DC * 1024], BF16))
            load_cast_resident(wv_bb, wv)
            xs = [ph.enter_context(nc.sbuf_tensor(_nm("x3s"), [128, 512], F32))
                  for _ in range(2)]
            xb = [ph.enter_context(nc.sbuf_tensor(_nm("x3b"), [128, 512], BF16))
                  for _ in range(2)]
            pv = [ph.enter_context(nc.psum_tensor(_nm("pv"), [128, 1024], F32))
                  for _ in range(4)]
            b = dict(n)
            with nc.Block() as blk:
                @blk.sync
                def _(sync):
                    for ts in range(8):
                        for dc in range(DC):
                            idx = ts * DC + dc
                            if idx >= 2:
                                W(sync, "ac", b["ac"] + idx - 1)
                            I(sync.dma_start(
                                xs[idx % 2][:],
                                xfm[dc][:, ts * 512:(ts + 1) * 512]), "ld", 16)

                @blk.scalar
                def _(scalar):
                    for ts in range(8):
                        for dc in range(DC):
                            idx = ts * DC + dc
                            W(scalar, "ld", b["ld"] + 16 * (idx + 1))
                            if idx >= 2:
                                W(scalar, "mm", b["mm"] + idx - 1)
                            I(scalar.copy(xb[idx % 2][:], xs[idx % 2][:]), "ac")

                @blk.tensor
                def _(tensor):
                    for ts in range(8):
                        for dc in range(DC):
                            idx = ts * DC + dc
                            W(tensor, "ac", b["ac"] + idx + 1)
                            if dc == 0 and ts >= 1:
                                W(tensor, "ev", b["ev"] + 4 * ts)
                            ins = None
                            for tcl in range(4):
                                for h in range(2):
                                    ins = tensor.matmul(
                                        pv[tcl][:, h * 512:(h + 1) * 512],
                                        xb[idx % 2][:, tcl * 128:(tcl + 1) * 128],
                                        wv_bb[:, dc * 1024 + h * 512:dc * 1024 + (h + 1) * 512],
                                        start=(dc == 0), stop=(dc == DC - 1))
                            I(ins, "mm")

                @blk.vector
                def _(vector):
                    for ts in range(8):
                        W(vector, "mm", b["mm"] + (ts + 1) * DC)
                        for tcl in range(4):
                            tc = ts * 4 + tcl
                            I(vector.tensor_copy(
                                v_sb[:, tc * 1024:(tc + 1) * 1024], pv[tcl][:]), "ev")

        if stop_after == "kv":
            _dump_and_stop([(dbg["dq"], qdr), (dbg["dk"], kdr)],
                           [(dbg["dv"], v_sb)])
            return nc

        with ExitStack() as ph:
            K_sb = [ph.enter_context(nc.sbuf_tensor(_nm("Ksb"), [128, S], BF16))
                    for _ in range(2)]
            Q_sb = [ph.enter_context(nc.sbuf_tensor(_nm("Qsb"), [128, T], BF16))
                    for _ in range(2)]
            k2_b = ph.enter_context(nc.sbuf_tensor("k2_b", [128, S], BF16))
            q2_b = ph.enter_context(nc.sbuf_tensor("q2_b", [128, T], BF16))
            kss_sb = ph.enter_context(nc.sbuf_tensor("kss_sb", [1, S], F32))
            kmx11 = ph.enter_context(nc.sbuf_tensor("kmx11", [1, 1], F32))
            qmx11 = ph.enter_context(nc.sbuf_tensor("qmx11", [1, 1], F32))
            prd11 = ph.enter_context(nc.sbuf_tensor("prd11", [1, 1], F32))
            sqq11 = ph.enter_context(nc.sbuf_tensor("sqq11", [1, 1], F32))
            ngb11 = ph.enter_context(nc.sbuf_tensor("ngb11", [1, 1], F32))
            negb = ph.enter_context(nc.sbuf_tensor("negb", [128, 1], F32))
            r_sb = ph.enter_context(nc.sbuf_tensor("r_sb", [1, T], F32))
            R_sb = ph.enter_context(nc.sbuf_tensor("R_sb", [128, T], F32))
            pU = [ph.enter_context(nc.sbuf_tensor(_nm("pU"), [128, T], BF16))
                  for _ in range(2)]
            av_sb = [ph.enter_context(nc.sbuf_tensor(_nm("av"), [128, T], BF16))
                     for _ in range(2)]
            psc = [ph.enter_context(nc.psum_tensor(_nm("psc"), [128, 1024], F32))
                   for _ in range(2)]
            paux = ph.enter_context(nc.psum_tensor("paux", [1, 1024], F32))
            pav = ph.enter_context(nc.psum_tensor("pav", [128, 1024], F32))
            b = dict(n)

            def pieces(kc):
                qlo = 128 * (kc // 4)
                fd = 1024 - qlo
                offs = []
                off = 0
                while off < fd:
                    w_ = min(512, fd - off)
                    offs.append((off, w_))
                    off += w_
                return qlo, fd, offs

            with nc.Block() as blk:
                @blk.sync
                def _(sync):
                    W(sync, "st", n["st"])
                    I(sync.dma_start(
                        msk[:].rearrange("p (c k) -> p c k", c=DC),
                        maskstk[:].rearrange("c p k -> p c k")), "ld", 16)
                    for kvh in range(NKV):
                        if kvh >= 2:
                            W(sync, "ssc", b["ssc"] + 128 * (kvh - 1))
                        I(sync.dma_start(K_sb[kvh % 2][:],
                                         kdr[kvh * 128:(kvh + 1) * 128, :]), "ld", 16)
                        for qr in range(4):
                            it = kvh * 4 + qr
                            if it >= 2:
                                W(sync, "ssc", b["ssc"] + 32 * (it - 1))
                            I(sync.dma_start(Q_sb[it % 2][:],
                                             qdr[it * 128:(it + 1) * 128, :]),
                              "ld", 16)

                @blk.scalar
                def _(scalar):
                    for kvh in range(NKV):
                        W(scalar, "ld", b["ld"] + 16 * (5 * kvh + 2))
                        if kvh >= 1:
                            W(scalar, "kA", b["kA"] + 8 * kvh)
                            W(scalar, "kC", b["kC"] + kvh)
                        I(scalar.activation(k2_b[:], K_sb[kvh % 2][:], AF.Square),
                          "kB")
                        for j in range(8):
                            W(scalar, "kA", b["kA"] + 8 * kvh + j + 1)
                            I(scalar.copy(kss_sb[0:1, j * 512:(j + 1) * 512],
                                          paux[0:1, (j % 2) * 512:(j % 2 + 1) * 512]),
                              "kB")
                        for qr in range(4):
                            it = kvh * 4 + qr
                            W(scalar, "ld", b["ld"] + 16 * (5 * kvh + 3 + qr))
                            W(scalar, "qB", b["qB"] + it)
                            I(scalar.activation(q2_b[:], Q_sb[it % 2][:], AF.Square),
                              "qA")
                            W(scalar, "pA", b["pA"] + it + 1)
                            I(scalar.activation(sqq11[:], prd11[:], AF.Sqrt), "qD")
                            W(scalar, "mm", b["mm"] + it + 1)
                            I(scalar.copy(negb[:], psc[0][:, 0:1]), "qD")
                            for kc in range(DC):
                                qlo, fd, offs = pieces(kc)
                                W(scalar, "ssc", b["ssc"] + 32 * it + kc + 1)
                                if kc >= 2:
                                    W(scalar, "scn", b["scn"] + 32 * it + kc - 1)
                                I(scalar.activation(
                                    pU[kc % 2][:, 0:fd], psc[kc % 2][:, 0:fd],
                                    AF.Exp, bias=negb[:]), "spr")
                            W(scalar, "rA", b["rA"] + it + 1)
                            W(scalar, "rC", b["rC"] + it)
                            I(scalar.copy(R_sb[:], psc[0][:, 0:1024]), "rB")

                @blk.tensor
                def _(tensor):
                    for kvh in range(NKV):
                        W(tensor, "ld", b["ld"] + 16 * (5 * kvh + 2))
                        W(tensor, "kB", b["kB"] + 9 * kvh + 1)
                        if kvh >= 1:
                            W(tensor, "rD", b["rD"] + 4 * kvh)
                        for j in range(8):
                            if j >= 2:
                                W(tensor, "kB", b["kB"] + 9 * kvh + j)
                            I(tensor.matmul(
                                paux[0:1, (j % 2) * 512:(j % 2 + 1) * 512],
                                c_oc[:], k2_b[:, j * 512:(j + 1) * 512],
                                start=True, stop=True), "kA")
                        for qr in range(4):
                            it = kvh * 4 + qr
                            W(tensor, "ld", b["ld"] + 16 * (5 * kvh + 3 + qr))
                            W(tensor, "qA", b["qA"] + it + 1)
                            if qr == 0:
                                W(tensor, "kB", b["kB"] + 9 * (kvh + 1))
                            else:
                                W(tensor, "rD", b["rD"] + it)
                            ins = None
                            for h in range(2):
                                ins = tensor.matmul(
                                    paux[0:1, h * 512:(h + 1) * 512],
                                    c_oc[:], q2_b[:, h * 512:(h + 1) * 512],
                                    start=True, stop=True)
                            I(ins, "qB")
                            W(tensor, "qC", b["qC"] + it + 1)
                            W(tensor, "rB", b["rB"] + it)
                            I(tensor.matmul(psc[0][:, 0:1], c_1f[:], ngb11[:],
                                            start=True, stop=True), "mm")
                            for kc in range(DC):
                                qlo, fd, offs = pieces(kc)
                                pp = psc[kc % 2]
                                if kc < 2:
                                    W(tensor, "qD", b["qD"] + 2 * it + 2)
                                else:
                                    W(tensor, "spr", b["spr"] + 32 * it + kc - 1)
                                for (off, w_) in offs:
                                    tensor.matmul(
                                        pp[:, off:off + w_],
                                        K_sb[kvh % 2][:, kc * 128:(kc + 1) * 128],
                                        Q_sb[it % 2][:, qlo + off:qlo + off + w_],
                                        start=True, stop=False)
                                I(tensor.matmul(
                                    pp[:, 0:128], c_id[:],
                                    msk[:, kc * 128:(kc + 1) * 128],
                                    start=False, stop=True), "ssc")
                                W(tensor, "spr", b["spr"] + 32 * it + kc + 1)
                                if kc == 0:
                                    W(tensor, "pA", b["pA"] + it + 1)
                                    W(tensor, "rC", b["rC"] + it)
                                for (off, w_) in offs:
                                    tensor.matmul(
                                        paux[0:1, qlo + off:qlo + off + w_],
                                        c_oc[:], pU[kc % 2][:, off:off + w_],
                                        start=(kc == 0), stop=(kc == DC - 1))
                                ins = None
                                for (off, w_) in offs:
                                    ins = tensor.matmul(
                                        pav[:, qlo + off:qlo + off + w_],
                                        v_sb[:, kc * 1024 + kvh * 128:kc * 1024 + (kvh + 1) * 128],
                                        pU[kc % 2][:, off:off + w_],
                                        start=(kc == 0), stop=(kc == DC - 1))
                                I(ins, "scn")
                            W(tensor, "rD", b["rD"] + it + 1)
                            ins = None
                            for h in range(2):
                                ins = tensor.matmul(
                                    psc[0][:, h * 512:(h + 1) * 512], c_1f[:],
                                    r_sb[0:1, h * 512:(h + 1) * 512],
                                    start=True, stop=True)
                            I(ins, "rA")

                @blk.vector
                def _(vector):
                    for kvh in range(NKV):
                        W(vector, "kB", b["kB"] + 9 * kvh + 9)
                        I(vector.reduce_max(kmx11[:], kss_sb[:], AX.X), "kC")
                        for qr in range(4):
                            it = kvh * 4 + qr
                            W(vector, "qB", b["qB"] + it + 1)
                            vector.reduce_max(qmx11[:], paux[0:1, 0:1024], AX.X)
                            I(vector.tensor_mul(prd11[:], qmx11[:], kmx11[:]), "pA")
                            W(vector, "qD", b["qD"] + 2 * it + 1)
                            I(vector.tensor_scalar_mul(ngb11[:], sqq11[:], -1.0),
                              "qC")
                            W(vector, "scn", b["scn"] + 32 * it + 32)
                            W(vector, "rA", b["rA"] + it)
                            I(vector.reciprocal(r_sb[:], paux[0:1, 0:1024]), "rD")
                            W(vector, "rB", b["rB"] + it + 1)
                            if it >= 2:
                                W(vector, "st", b["st"] + 16 * (it - 1))
                            I(vector.tensor_mul(av_sb[it % 2][:], pav[:], R_sb[:]),
                              "rC")

                @blk.gpsimd
                def _(gpsimd):
                    for it in range(32):
                        W(gpsimd, "rC", b["rC"] + it + 1)
                        I(gpsimd.dma_start(audr[it * 128:(it + 1) * 128, :],
                                           av_sb[it % 2][:]), "st", 16)

        if stop_after == "attn":
            _dump_and_stop([(dbg["dau"], audr)])
            return nc

        ph35.close()

        g_sb_ref = [None]

        def big_proj_phase(wsrc, rhs_sb, res_mode, res_dram, mdst, rdst, lora):
            with ExitStack() as ph:
                w_stg = ph.enter_context(
                    nc.sbuf_tensor(_nm("Pws"), [128, DC * 128], F32))
                w_bb = [ph.enter_context(
                    nc.sbuf_tensor(_nm("Pwb"), [128, DC * 256], BF16))
                    for _ in range(2)]
                x0s = ([ph.enter_context(
                    nc.sbuf_tensor(_nm("Px0"), [128, 1024], F32))
                    for _ in range(2)] if res_mode == "dram" else None)
                xp = [ph.enter_context(
                    nc.sbuf_tensor(_nm("Pxp"), [128, 1024], BF16))
                    for _ in range(2)]
                sq_b = ph.enter_context(nc.sbuf_tensor(_nm("Psq"), [128, 1024], BF16))
                fin = [ph.enter_context(nc.sbuf_tensor(_nm("Pf"), [1, 1024], F32))
                       for _ in range(6)]
                pw = [ph.enter_context(nc.psum_tensor(_nm("Ppw"), [128, 1024], F32))
                      for _ in range(2)]
                pss = ph.enter_context(nc.psum_tensor(_nm("Pps"), [1, 1024], F32))
                psq = ph.enter_context(nc.psum_tensor(_nm("Ppq"), [1, 1024], F32))
                b = dict(n)
                nload = 4 if res_mode == "dram" else 2
                with nc.Block() as blk:
                    @blk.sync
                    def _(sync):
                        W(sync, "st", n["st"])
                        for og in range(16):
                            for hf in range(2):
                                W(sync, "ev", b["ev"] + 2 * og + hf)
                                I(sync.dma_start(
                                    w_stg[:].rearrange("p (c f) -> p c f", c=16),
                                    wsrc[og, 16 * hf:16 * (hf + 1)]
                                    .rearrange("c p f -> p c f")), "ld", 16)
                            if res_mode == "dram":
                                for o in range(2):
                                    oc = 2 * og + o
                                    if oc >= 2:
                                        W(sync, "po", b["po"] + oc - 1)
                                    I(sync.dma_start(x0s[oc % 2][:], xown[oc]),
                                      "ld", 16)

                    @blk.vector
                    def _(vector):
                        for og in range(16):
                            for hf in range(2):
                                W(vector, "ld",
                                  b["ld"] + 16 * (nload * og + hf + 1))
                                if og >= 2 and hf == 0:
                                    W(vector, "mm", b["mm"] + og - 1)
                                I(vector.tensor_copy(
                                    w_bb[og % 2][:, hf * 4096:(hf + 1) * 4096],
                                    w_stg[:]), "ev")
                            for o in range(2):
                                oc = 2 * og + o
                                W(vector, "mm", b["mm"] + og + 1)
                                if res_mode == "dram":
                                    W(vector, "ld",
                                      b["ld"] + 16 * (nload * og + 2 + o + 1))
                                if oc >= 2:
                                    W(vector, "sA", b["sA"] + oc - 1)
                                    W(vector, "st", b["st"] + 16 * (oc - 1))
                                if res_mode == "dram":
                                    I(vector.tensor_add(xp[oc % 2][:], pw[o][:],
                                                        x0s[oc % 2][:]), "po")
                                else:
                                    I(vector.tensor_add(
                                        xp[oc % 2][:], pw[o][:],
                                        rhs_sb[:, oc * 1024:(oc + 1) * 1024]), "po")
                        W(vector, "sA", b["sA"] + 32)
                        I(vector.tensor_scalar_mul(fin[0][:], pss[:], 1.0 / D), "ev")
                        I(vector.tensor_scalar_mul(fin[1][:], psq[:], 1.0 / D), "ev")
                        W(vector, "ac", b["ac"] + 33)
                        I(vector.tensor_sub(fin[3][:], fin[1][:], fin[2][:]), "ev")
                        W(vector, "ac", b["ac"] + 34)
                        I(vector.reciprocal(fin[5][:], fin[4][:]), "ev")

                    @blk.tensor
                    def _(tensor):
                        for og in range(16):
                            W(tensor, "ev", b["ev"] + 2 * (og + 1))
                            if og >= 1:
                                W(tensor, "po", b["po"] + 2 * og)
                            ins = None
                            for dc in range(DC):
                                for o in range(2):
                                    for h in range(2):
                                        ins = tensor.matmul(
                                            pw[o][:, h * 512:(h + 1) * 512],
                                            w_bb[og % 2][:, dc * 256 + o * 128:dc * 256 + (o + 1) * 128],
                                            rhs_sb[:, dc * 1024 + h * 512:dc * 1024 + (h + 1) * 512],
                                            start=(dc == 0),
                                            stop=(dc == DC - 1 and not lora))
                            if lora:
                                for o in range(2):
                                    oc = 2 * og + o
                                    for h in range(2):
                                        tensor.matmul(
                                            pw[o][:, h * 512:(h + 1) * 512],
                                            lb_b[:, oc * 128:(oc + 1) * 128],
                                            g_sb_ref[0][:, h * 512:(h + 1) * 512],
                                            start=False, stop=False)
                                    for h in range(2):
                                        ins = tensor.matmul(
                                            pw[o][:, h * 512:(h + 1) * 512],
                                            bb_b[0:1, oc * 128:(oc + 1) * 128],
                                            c_or[:],
                                            start=False, stop=True)
                            I(ins, "mm")
                            for o in range(2):
                                oc = 2 * og + o
                                W(tensor, "ac", b["ac"] + oc + 1)
                                for h in range(2):
                                    tensor.matmul(
                                        pss[0:1, h * 512:(h + 1) * 512], c_oc[:],
                                        xp[oc % 2][:, h * 512:(h + 1) * 512],
                                        start=(oc == 0), stop=(oc == 31))
                                ins = None
                                for h in range(2):
                                    ins = tensor.matmul(
                                        psq[0:1, h * 512:(h + 1) * 512], c_oc[:],
                                        sq_b[:, h * 512:(h + 1) * 512],
                                        start=(oc == 0), stop=(oc == 31))
                                I(ins, "sA")
                        W(tensor, "po", b["po"] + 32)
                        W(tensor, "ev", b["ev"] + 36)
                        for h in range(2):
                            tensor.matmul(pw[0][:, h * 512:(h + 1) * 512], c_1f[:],
                                          fin[0][0:1, h * 512:(h + 1) * 512],
                                          start=True, stop=True)
                        ins = None
                        for h in range(2):
                            ins = tensor.matmul(pw[1][:, h * 512:(h + 1) * 512],
                                                c_1f[:],
                                                fin[5][0:1, h * 512:(h + 1) * 512],
                                                start=True, stop=True)
                        I(ins, "mm")

                    @blk.scalar
                    def _(scalar):
                        for og in range(16):
                            for o in range(2):
                                oc = 2 * og + o
                                W(scalar, "po", b["po"] + oc + 1)
                                W(scalar, "sA", b["sA"] + oc)
                                I(scalar.activation(sq_b[:], xp[oc % 2][:],
                                                    AF.Square), "ac")
                        W(scalar, "ev", b["ev"] + 33)
                        I(scalar.activation(fin[2][:], fin[0][:], AF.Square), "ac")
                        W(scalar, "ev", b["ev"] + 35)
                        I(scalar.activation(fin[4][:], fin[3][:], AF.Sqrt,
                                            bias=eps11[:]), "ac")
                        W(scalar, "mm", b["mm"] + 17)
                        I(scalar.copy(mdst[:], pw[0][:]), "ac")
                        I(scalar.copy(rdst[:], pw[1][:]), "ac")

                    @blk.gpsimd
                    def _(gpsimd):
                        for oc in range(DC):
                            W(gpsimd, "po", b["po"] + oc + 1)
                            I(gpsimd.dma_start(res_dram[oc], xp[oc % 2][:]),
                              "st", 16)

        with ExitStack() as ph6:
            au_sb = ph6.enter_context(nc.sbuf_tensor("au_sb", [128, DC * 1024],
                                                     BF16))
            b6 = dict(n)
            with nc.Block() as blk:
                @blk.sync
                def _(sync):
                    W(sync, "st", n["st"])
                    I(sync.dma_start(
                        au_sb[:].rearrange("p (c t) -> p c t", c=DC),
                        audr[:].rearrange("(c p) t -> p c t", p=128)), "ld", 16)

                @blk.tensor
                def _(tensor):
                    W(tensor, "ld", b6["ld"] + 16)
            big_proj_phase(wo, au_sb, "dram", x1dr, m1, r1, lora=False)

        ph78 = top.enter_context(ExitStack())
        m2 = ph78.enter_context(nc.sbuf_tensor("m2", [128, T], F32))
        r2 = ph78.enter_context(nc.sbuf_tensor("r2", [128, T], F32))
        with ExitStack() as ph7:
            la_b = ph7.enter_context(nc.sbuf_tensor("la_b", [128, DC * 128], BF16))
            lb_b = ph7.enter_context(nc.sbuf_tensor("lb_b", [128, DC * 128], BF16))
            with ExitStack() as ph:
                la_f = ph.enter_context(nc.sbuf_tensor("la_f", [128, DC * 128],
                                                       F32))
                bl = dict(n)
                with nc.Block() as blk:
                    @blk.sync
                    def _(sync):
                        I(sync.dma_start(
                            la_f[:].rearrange("p (c f) -> p c f", c=DC),
                            la[:].rearrange("c p f -> p c f")), "ld", 16)
                        W(sync, "ld", bl["ld"] + 16)

                    @blk.vector
                    def _(vector):
                        W(vector, "ld", bl["ld"] + 16)
                        vector.tensor_copy(la_b[:], la_f[:])
                with nc.Block() as blk:
                    @blk.sync
                    def _(sync):
                        I(sync.dma_start(
                            la_f[:].rearrange("p (c f) -> p c f", c=DC),
                            lb[:].rearrange("c f p -> f c p")), "ld", 16)
                        W(sync, "ld", bl["ld"] + 32)

                    @blk.vector
                    def _(vector):
                        W(vector, "ld", bl["ld"] + 32)
                        vector.tensor_copy(lb_b[:], la_f[:])
            x1_sb = ph7.enter_context(nc.sbuf_tensor("x1_sb", [128, DC * 1024],
                                                     BF16))
            with ExitStack() as ph:
                xps = [ph.enter_context(nc.sbuf_tensor(_nm("l1a"), [128, 1024],
                                                       BF16)) for _ in range(2)]
                tmp = [ph.enter_context(nc.sbuf_tensor(_nm("l1t"), [128, 1024],
                                                       F32)) for _ in range(2)]
                b = dict(n)
                with nc.Block() as blk:
                    @blk.sync
                    def _(sync):
                        W(sync, "st", n["st"])
                        for oc in range(DC):
                            if oc >= 2:
                                W(sync, "ev", b["ev"] + oc - 1)
                            I(sync.dma_start(xps[oc % 2][:], x1dr[oc]), "ld", 16)

                    @blk.vector
                    def _(vector):
                        for oc in range(DC):
                            W(vector, "ld", b["ld"] + 16 * (oc + 1))
                            if oc >= 2:
                                W(vector, "po", b["po"] + oc - 1)
                            I(vector.tensor_sub(tmp[oc % 2][:], xps[oc % 2][:],
                                                m1[:]), "ev")

                    @blk.gpsimd
                    def _(gpsimd):
                        for oc in range(DC):
                            W(gpsimd, "ev", b["ev"] + oc + 1)
                            I(gpsimd.tensor_mul(
                                x1_sb[:, oc * 1024:(oc + 1) * 1024],
                                tmp[oc % 2][:], r1[:]), "po")

            if stop_after == "x1":
                _dump_and_stop([], [(dbg["dx1"], x1_sb)])
                return nc

            g_sb = ph7.enter_context(nc.sbuf_tensor("g_sb", [128, T], BF16))
            g_sb_ref[0] = g_sb
            with ExitStack() as ph:
                z_sb = ph.enter_context(nc.sbuf_tensor("z_sb", [128, 64], F32))
                zs = ph.enter_context(nc.sbuf_tensor("zs", [128, 64], F32))
                cum = ph.enter_context(nc.sbuf_tensor("cum", [128, 64], F32))
                tmp6 = ph.enter_context(nc.sbuf_tensor("tmp6", [128, 64], F32))
                ind = ph.enter_context(nc.sbuf_tensor("ind", [128, 64], F32))
                kz8 = ph.enter_context(nc.sbuf_tensor("kz8", [128, 8], F32))
                ts8 = ph.enter_context(nc.sbuf_tensor("ts8", [128, 8], F32))
                tau8 = ph.enter_context(nc.sbuf_tensor("tau8", [128, 8], F32))
                kzr = ph.enter_context(nc.sbuf_tensor("kzr", [128, 8], F32))
                trep = ph.enter_context(nc.sbuf_tensor("trep", [128, 64], F32))
                w_f = ph.enter_context(nc.sbuf_tensor("w_f", [128, 64], F32))
                w_b2 = ph.enter_context(nc.sbuf_tensor("w_b2", [128, 64], BF16))
                weT = ph.enter_context(nc.sbuf_tensor("weT", [E, T], BF16))
                wer = ph.enter_context(nc.sbuf_tensor("wer", [128, T], BF16))
                plg = ph.enter_context(nc.psum_tensor("plg", [128, E], F32))
                ptr8 = ph.enter_context(nc.psum_tensor("ptr8", [E, 128], BF16))
                pwer = ph.enter_context(nc.psum_tensor("pwer", [128, 1024], F32))
                phh = ph.enter_context(nc.psum_tensor("phh", [128, 1024], F32))

                b = dict(n)
                with nc.Block() as blk:
                    @blk.tensor
                    def _(tensor):
                        for tc in range(8):
                            if tc >= 1:
                                W(tensor, "ac", b["ac"] + tc)
                            for dc in range(DC):
                                tensor.matmul(
                                    plg[:],
                                    x1_sb[:, dc * 1024 + tc * 128:dc * 1024 + (tc + 1) * 128],
                                    wr_b[:, dc * E:(dc + 1) * E],
                                    start=(dc == 0), stop=False)
                            I(tensor.matmul(plg[:], c_1b[:], br_b[:],
                                            start=False, stop=True), "mm")

                    @blk.scalar
                    def _(scalar):
                        for tc in range(8):
                            W(scalar, "mm", b["mm"] + tc + 1)
                            I(scalar.copy(z_sb[:, tc * 8:(tc + 1) * 8], plg[:]),
                              "ac")

                def v3(t_):
                    return t_[:].rearrange("p (g e) -> p g e", e=8)

                def v5(t_):
                    return t_[:].rearrange("p (g i j k) -> p g i j k",
                                           i=2, j=2, k=2)

                def sva(t_, off):
                    return t_[:].rearrange("p (g a e) -> p g a e",
                                           a=2, e=4)[:, :, :, off:off + 1]

                def s6v(t_, off):
                    return v3(t_)[:, :, off:off + 6:2]

                ops = []

                def add(fn, dve_only=True):
                    ops.append((fn, False))

                def addc(fn):
                    ops.append((fn, True))

                addc(lambda e: e.tensor_copy(zs[:], z_sb[:]))
                for (va, vb) in [
                    (lambda t: v5(t)[:, :, :, :, 0], lambda t: v5(t)[:, :, :, :, 1]),
                    (lambda t: v5(t)[:, :, :, 0, :], lambda t: v5(t)[:, :, :, 1, :]),
                    (lambda t: sva(t, 1), lambda t: sva(t, 2)),
                    (lambda t: v5(t)[:, :, 0, :, :], lambda t: v5(t)[:, :, 1, :, :]),
                    (lambda t: v3(t)[:, :, 2:4], lambda t: v3(t)[:, :, 4:6]),
                    (lambda t: s6v(t, 1), lambda t: s6v(t, 2)),
                ]:
                    add(lambda e, va=va, vb=vb: e.tensor_tensor(
                        va(tmp6), va(zs), vb(zs), ALU.max))
                    add(lambda e, va=va, vb=vb: e.tensor_tensor(
                        vb(tmp6), va(zs), vb(zs), ALU.min))
                    addc(lambda e, va=va, vb=vb: e.tensor_copy(va(zs), va(tmp6)))
                    addc(lambda e, va=va, vb=vb: e.tensor_copy(vb(zs), vb(tmp6)))
                addc(lambda e: e.tensor_copy(cum[:], zs[:]))
                for d_ in (1, 2, 4):
                    add(lambda e, d_=d_: e.tensor_tensor(
                        v3(tmp6)[:, :, d_:8], v3(cum)[:, :, d_:8],
                        v3(cum)[:, :, 0:8 - d_], ALU.add))
                    addc(lambda e, d_=d_: e.tensor_copy(
                        v3(cum)[:, :, d_:8], v3(tmp6)[:, :, d_:8]))
                add(lambda e: e.tensor_mul(tmp6[:], zs[:], c_kv[:]))
                add(lambda e: e.tensor_sub(tmp6[:], tmp6[:], cum[:]))
                add(lambda e: e.tensor_single_scalar(ind[:], tmp6[:], -1.0,
                                                     ALU.is_gt))
                add(lambda e: e.tensor_reduce(kz8[:], v3(ind), AX.X, ALU.add))
                addc(lambda e: e.tensor_copy(tmp6[:], ind[:]))
                add(lambda e: e.tensor_tensor(
                    v3(tmp6)[:, :, 0:7], v3(ind)[:, :, 0:7], v3(ind)[:, :, 1:8],
                    ALU.subtract))
                add(lambda e: e.tensor_mul(tmp6[:], tmp6[:], cum[:]))
                add(lambda e: e.tensor_reduce(ts8[:], v3(tmp6), AX.X, ALU.add))
                add(lambda e: e.tensor_single_scalar(ts8[:], ts8[:], 1.0,
                                                     ALU.subtract))
                add(lambda e: e.reciprocal(kzr[:], kz8[:]), dve_only=True)
                add(lambda e: e.tensor_mul(tau8[:], ts8[:], kzr[:]))
                addc(lambda e: e.tensor_copy(v3(trep)[:, :, 0], tau8[:]))
                addc(lambda e: e.tensor_copy(v3(trep)[:, :, 1], tau8[:]))
                addc(lambda e: e.tensor_copy(v3(trep)[:, :, 2:4],
                                            v3(trep)[:, :, 0:2]))
                addc(lambda e: e.tensor_copy(v3(trep)[:, :, 4:8],
                                            v3(trep)[:, :, 0:4]))
                add(lambda e: e.tensor_sub(w_f[:], z_sb[:], trep[:]))
                add(lambda e: e.tensor_scalar_max(w_f[:], w_f[:], 0.0))

                sched = []
                last = 1
                for (fn, pool_ok) in ops:
                    if pool_ok and last == 0:
                        sched.append((1, fn))
                    else:
                        if last == 0:
                            sched.append((1, None))
                        sched.append((0, fn))
                    last = sched[-1][0]
                bnc = ph.enter_context(nc.sbuf_tensor("bnc", [1, 4], F32))
                b = dict(n)
                nv = sum(1 for e_, _ in sched if e_ == 0)
                np_ = sum(1 for e_, _ in sched if e_ == 1)
                with nc.Block() as blk:
                    @blk.vector
                    def _(vector):
                        cv = cp = 0
                        for (e_, fn) in sched:
                            if e_ == 0:
                                W(vector, "po", b["po"] + cp)
                                fn_ = fn if fn else (
                                    lambda e: e.tensor_copy(bnc[0:1, 0:2],
                                                            bnc[0:1, 2:4]))
                                I(fn_(vector), "ev")
                                cv += 1
                            else:
                                cp += 1

                    @blk.gpsimd
                    def _(gpsimd):
                        cv = cp = 0
                        for (e_, fn) in sched:
                            if e_ == 1:
                                W(gpsimd, "ev", b["ev"] + cv)
                                fn_ = fn if fn else (
                                    lambda e: e.tensor_copy(bnc[0:1, 0:2],
                                                            bnc[0:1, 2:4]))
                                I(fn_(gpsimd), "po")
                                cp += 1
                            else:
                                cv += 1

                if stop_after == "w":
                    _dump_and_stop([], [(dbg["dz"], z_sb), (dbg["dw"], w_f)])
                    return nc

                b = dict(n)
                with nc.Block() as blk:
                    @blk.scalar
                    def _(scalar):
                        I(scalar.copy(w_b2[:], w_f[:]), "ac")
                        for tc in range(8):
                            W(scalar, "mm", b["mm"] + tc + 1)
                            I(scalar.copy(weT[0:E, tc * 128:(tc + 1) * 128],
                                          ptr8[:]), "ac")
                        W(scalar, "mm", b["mm"] + 9)
                        I(scalar.copy(wer[:], pwer[:]), "ac")

                    @blk.tensor
                    def _(tensor):
                        for tc in range(8):
                            W(tensor, "ac", b["ac"] + 1 + tc)
                            I(tensor.transpose(ptr8[:],
                                               w_b2[:, tc * 8:(tc + 1) * 8],
                                               c_id[:]), "mm")
                        W(tensor, "ac", b["ac"] + 9)
                        ins = None
                        for h in range(2):
                            ins = tensor.matmul(
                                pwer[:, h * 512:(h + 1) * 512], c_e8[:],
                                weT[0:E, h * 512:(h + 1) * 512],
                                start=True, stop=True)
                        I(ins, "mm")
                        W(tensor, "ac", b["ac"] + 10)
                        ins = None
                        for dc in range(DC):
                            for h in range(2):
                                ins = tensor.matmul(
                                    phh[:, h * 512:(h + 1) * 512],
                                    la_b[:, dc * 128:(dc + 1) * 128],
                                    x1_sb[:, dc * 1024 + h * 512:dc * 1024 + (h + 1) * 512],
                                    start=(dc == 0), stop=(dc == DC - 1))
                        I(ins, "mm")

                    @blk.vector
                    def _(vector):
                        W(vector, "mm", b["mm"] + 10)
                        I(vector.tensor_mul(g_sb[:], phh[:], wer[:]), "ev")

            if stop_after == "g":
                _dump_and_stop([], [(dbg["dz"], z_sb), (dbg["dw"], w_f),
                                    (dbg["dg"], g_sb)])
                return nc

            big_proj_phase(wb, x1_sb, "sbuf", x2dr, m2, r2, lora=True)

        if stop_after == "x2":
            _dump_and_stop([(dbg["dx2"], x2dr)],
                           [(dbg["dm2"], m2), (dbg["dr2"], r2)])
            return nc

        with ExitStack() as ph:
            xps = [ph.enter_context(nc.sbuf_tensor(_nm("l2a"), [128, 1024], BF16))
                   for _ in range(2)]
            tmp = [ph.enter_context(nc.sbuf_tensor(_nm("l2t"), [128, 1024], F32))
                   for _ in range(2)]
            oc_f = [ph.enter_context(nc.sbuf_tensor(_nm("l2o"), [128, 1024], F32))
                    for _ in range(2)]
            b = dict(n)
            with nc.Block() as blk:
                @blk.sync
                def _(sync):
                    W(sync, "st", n["st"])
                    for oc in range(DC):
                        if oc >= 2:
                            W(sync, "ev", b["ev"] + oc - 1)
                        I(sync.dma_start(xps[oc % 2][:], x2dr[oc]), "ld", 16)
                        if oc >= 2:
                            W(sync, "po", b["po"] + oc - 1)
                            I(sync.dma_start(
                                out[(oc - 2) * 128:(oc - 1) * 128, :],
                                oc_f[oc % 2][:]), "st", 16)
                    for oc in range(DC - 2, DC):
                        W(sync, "po", b["po"] + oc + 1)
                        I(sync.dma_start(out[oc * 128:(oc + 1) * 128, :],
                                         oc_f[oc % 2][:]), "st", 16)
                    W(sync, "st", b["st"] + 16 * DC)

                @blk.vector
                def _(vector):
                    for oc in range(DC):
                        W(vector, "ld", b["ld"] + 16 * (oc + 1))
                        if oc >= 2:
                            W(vector, "po", b["po"] + oc - 1)
                        I(vector.tensor_sub(tmp[oc % 2][:], xps[oc % 2][:], m2[:]),
                          "ev")

                @blk.gpsimd
                def _(gpsimd):
                    for oc in range(DC):
                        W(gpsimd, "ev", b["ev"] + oc + 1)
                        if oc >= 2:
                            W(gpsimd, "st", b["st"] + 16 * (oc - 1))
                        I(gpsimd.tensor_mul(oc_f[oc % 2][:], tmp[oc % 2][:],
                                            r2[:]), "po")

    return nc


def kernel(**inputs):
    in_maps, own = _stage_inputs(inputs)
    nc = build_program()
    res = run_bass_kernel_spmd(nc, in_maps, core_ids=list(range(NCORE)))
    outp = np.zeros((B, S, D), dtype=np.float32)
    for c in range(NCORE):
        b_, rows = own[c]
        o = res.results[c]["out"]
        outp[b_][rows] = o.T
    return outp


# version 11
# speedup vs baseline: 1.3812x; 1.3812x over previous
import numpy as np
import ml_dtypes
from contextlib import ExitStack

import concourse.bass as bass
import concourse.mybir as mybir
from concourse.bass_utils import run_bass_kernel_spmd

F32 = mybir.dt.float32
BF16 = mybir.dt.bfloat16
AF = mybir.ActivationFunctionType
ALU = mybir.AluOpType
AX = mybir.AxisListType
BF = ml_dtypes.bfloat16

B, S, D = 2, 4096, 4096
NKV = 8
E, R = 8, 16
NCORE = 8
T = 1024
DC = 32
SCALE = float(128) ** -0.5
NEG = -1.0e30
EPS = 1e-5


def _stage_inputs(inputs):
    x = np.asarray(inputs["L0_attn_hidden_states"], dtype=np.float32)
    Wq = np.asarray(inputs["L0_attn_Wq"], dtype=np.float32)
    Wk = np.asarray(inputs["L0_attn_Wk"], dtype=np.float32)
    Wv = np.asarray(inputs["L0_attn_Wv"], dtype=np.float32)
    Wo = np.asarray(inputs["L0_attn_Wo"], dtype=np.float32)
    Wb = np.asarray(inputs["L0_ffn_W_base"], dtype=np.float32)
    bb = np.asarray(inputs["L0_ffn_b_base"], dtype=np.float32)
    Wr = np.asarray(inputs["L0_ffn_W_router"], dtype=np.float32)
    br = np.asarray(inputs["L0_ffn_b_router"], dtype=np.float32)
    lA = np.asarray(inputs["L0_ffn_lora_A"], dtype=np.float32)
    lB = np.asarray(inputs["L0_ffn_lora_B"], dtype=np.float32)

    def qg_major(W_din_dout):
        w = W_din_dout.reshape(DC, 128, DC, 128)
        w = w.transpose(2, 0, 1, 3)
        w = w.reshape(16, 2, DC, 128, 128).transpose(0, 2, 3, 1, 4)
        return np.ascontiguousarray(w.reshape(16, DC, 128, 256))

    kvec = np.ascontiguousarray(np.broadcast_to(
        np.tile(np.arange(1, 9, dtype=np.float32), 8)[None, :], (128, 64)))

    shared = {
        "wq": qg_major(Wq),
        "wo": qg_major(Wo),
        "wb": qg_major(np.ascontiguousarray(Wb.T)),
        "wk": np.ascontiguousarray(Wk.reshape(DC, 128, 1024)),
        "wv": np.ascontiguousarray(Wv.reshape(DC, 128, 1024)),
        "wr": np.ascontiguousarray(Wr.T.reshape(DC, 128, E)),
        "la": np.ascontiguousarray(lA.transpose(1, 0, 2).reshape(DC, 128, E * R)),
        "lb": np.ascontiguousarray(lB.reshape(E * R, DC, 128).transpose(1, 0, 2)),
        "bbs": np.ascontiguousarray(bb.reshape(1, D)),
        "brs": np.ascontiguousarray(br.reshape(1, E)),
        "idm": np.eye(128, dtype=np.float32).astype(BF),
        "onescol": np.ones((128, 1), dtype=BF),
        "ones1f": np.ones((1, 128), dtype=np.float32),
        "ones1b": np.ones((1, 128), dtype=BF),
        "onesrow": np.ones((1, 512), dtype=BF),
        "e8m": np.kron(np.eye(E, dtype=np.float32),
                       np.ones((1, R), dtype=np.float32)).astype(BF),
        "kvecf": kvec,
    }

    xfm = [np.ascontiguousarray(x[bb_].T).reshape(DC, 128, S) for bb_ in range(B)]

    in_maps, own = [], []
    for c in range(NCORE):
        b_, i = c // 4, c % 4
        rows = np.concatenate([np.arange((4 * s + i) * 128, (4 * s + i + 1) * 128)
                               for s in range(8)])
        own.append((b_, rows))
        xo = np.ascontiguousarray(x[b_][rows].T).reshape(DC, 128, T)
        mk = np.zeros((DC, 128, 128), dtype=np.float32)
        for kc in range(DC):
            g = 4 * (kc // 4) + i
            if kc > g:
                mk[kc, :, :] = NEG
            elif kc == g:
                kk = np.arange(128)[:, None]
                qq = np.arange(128)[None, :]
                mk[kc] = np.where(kk <= qq, 0.0, NEG)
        m = dict(shared)
        m["xfm"] = xfm[b_]
        m["xown"] = xo
        m["maskstk"] = np.ascontiguousarray(mk.astype(BF))
        in_maps.append(m)
    return in_maps, own


def build_program(stop_after="all"):
    nc = bass.Bass("TRN2", target_bir_lowering=False)

    def din(name, shape, dt=F32):
        return nc.dram_tensor(name, shape, dt, kind="ExternalInput").ap()

    xfm = din("xfm", [DC, 128, S])
    xown = din("xown", [DC, 128, T])
    wq = din("wq", [16, DC, 128, 256])
    wo = din("wo", [16, DC, 128, 256])
    wb = din("wb", [16, DC, 128, 256])
    wk = din("wk", [DC, 128, 1024])
    wv = din("wv", [DC, 128, 1024])
    wr = din("wr", [DC, 128, E])
    la = din("la", [DC, 128, E * R])
    lb = din("lb", [DC, E * R, 128])
    bbs = din("bbs", [1, D])
    brs = din("brs", [1, E])
    idm = din("idm", [128, 128], BF16)
    onescol = din("onescol", [128, 1], BF16)
    ones1f = din("ones1f", [1, 128])
    ones1b = din("ones1b", [1, 128], BF16)
    onesrow = din("onesrow", [1, 512], BF16)
    e8m = din("e8m", [E, E * R], BF16)
    maskstk = din("maskstk", [DC, 128, 128], BF16)
    kvecf = din("kvecf", [128, 64])

    qdr = nc.dram_tensor("qdr", [D, T], BF16, kind="Internal").ap()
    kdr = nc.dram_tensor("kdr", [1024, S], BF16, kind="Internal").ap()
    audr = nc.dram_tensor("audr", [D, T], BF16, kind="Internal").ap()
    x1dr = nc.dram_tensor("x1dr", [DC, 128, T], BF16, kind="Internal").ap()
    x2dr = nc.dram_tensor("x2dr", [DC, 128, T], BF16, kind="Internal").ap()
    out = nc.dram_tensor("out", [D, T], F32, kind="ExternalOutput").ap()

    uid = [0]

    def _nm(p):
        uid[0] += 1
        return f"{p}{uid[0]}"

    dbg = {}
    if stop_after in ("q", "kv"):
        dbg["dq"] = nc.dram_tensor("dq", [D, T], BF16, kind="ExternalOutput").ap()
    if stop_after == "kv":
        dbg["dk"] = nc.dram_tensor("dk", [1024, S], BF16,
                                   kind="ExternalOutput").ap()
        dbg["dv"] = nc.dram_tensor("dv", [128, DC * 1024], BF16,
                                   kind="ExternalOutput").ap()
    if stop_after == "attn":
        dbg["dau"] = nc.dram_tensor("dau", [D, T], BF16,
                                    kind="ExternalOutput").ap()
    if stop_after == "x1":
        dbg["dx1"] = nc.dram_tensor("dx1", [128, DC * 1024], BF16,
                                    kind="ExternalOutput").ap()
    if stop_after in ("w", "g"):
        dbg["dz"] = nc.dram_tensor("dz", [128, 64], F32,
                                   kind="ExternalOutput").ap()
        dbg["dw"] = nc.dram_tensor("dw", [128, 64], F32,
                                   kind="ExternalOutput").ap()
    if stop_after == "g":
        dbg["dg"] = nc.dram_tensor("dg", [128, T], BF16,
                                   kind="ExternalOutput").ap()
    if stop_after == "x2":
        dbg["dx2"] = nc.dram_tensor("dx2", [DC, 128, T], BF16,
                                    kind="ExternalOutput").ap()
        dbg["dm2"] = nc.dram_tensor("dm2", [128, T], F32,
                                    kind="ExternalOutput").ap()
        dbg["dr2"] = nc.dram_tensor("dr2", [128, T], F32,
                                    kind="ExternalOutput").ap()

    with ExitStack() as top:
        c_id = top.enter_context(nc.sbuf_tensor("c_id", [128, 128], BF16))
        c_oc = top.enter_context(nc.sbuf_tensor("c_oc", [128, 1], BF16))
        c_1f = top.enter_context(nc.sbuf_tensor("c_1f", [1, 128], F32))
        c_1b = top.enter_context(nc.sbuf_tensor("c_1b", [1, 128], BF16))
        c_or = top.enter_context(nc.sbuf_tensor("c_or", [1, 512], BF16))
        c_e8 = top.enter_context(nc.sbuf_tensor("c_e8", [E, E * R], BF16))
        c_kv = top.enter_context(nc.sbuf_tensor("c_kv", [128, 64], F32))
        wr_b = top.enter_context(nc.sbuf_tensor("wr_b", [128, DC * E], BF16))
        bb_b = top.enter_context(nc.sbuf_tensor("bb_b", [1, D], BF16))
        br_b = top.enter_context(nc.sbuf_tensor("br_b", [1, E], BF16))
        m1 = top.enter_context(nc.sbuf_tensor("m1", [128, T], F32))
        r1 = top.enter_context(nc.sbuf_tensor("r1", [128, T], F32))
        eps11 = top.enter_context(nc.sbuf_tensor("eps11", [1, 1], F32))
        qmax_sb = top.enter_context(nc.sbuf_tensor("qmax_sb", [1, 32], F32))
        kmax_sb = top.enter_context(nc.sbuf_tensor("kmax_sb", [1, 8], F32))

        sems = {}
        for nm in ["ld", "st", "mm", "ac", "ev", "po",
                   "ssc", "spr", "scn", "kA", "kB", "kC",
                   "qA", "qB", "qC", "qD", "pA",
                   "rA", "rB", "rC", "rD", "sA"]:
            sems[nm] = top.enter_context(nc.semaphore(nm))
        n = {k: 0 for k in sems}

        def I(instr, nm, amt=1):
            instr.then_inc(sems[nm], amt)
            n[nm] += amt

        def W(eng, nm, val):
            if val > 0:
                eng.wait_ge(sems[nm], val)

        with ExitStack() as ph:
            wr_f = ph.enter_context(nc.sbuf_tensor("wr_f", [128, DC * E], F32))
            bb_f = ph.enter_context(nc.sbuf_tensor("bb_f", [1, D], F32))
            br_f = ph.enter_context(nc.sbuf_tensor("br_f", [1, E], F32))
            with nc.Block() as blk:
                @blk.sync
                def _(sync):
                    I(sync.dma_start(c_id[:], idm[:]), "ld", 16)
                    I(sync.dma_start(c_oc[:], onescol[:]), "ld", 16)
                    I(sync.dma_start(c_1f[:], ones1f[:]), "ld", 16)
                    I(sync.dma_start(c_1b[:], ones1b[:]), "ld", 16)
                    I(sync.dma_start(c_or[:], onesrow[:]), "ld", 16)
                    I(sync.dma_start(c_e8[:], e8m[:]), "ld", 16)
                    I(sync.dma_start(c_kv[:], kvecf[:]), "ld", 16)
                    I(sync.dma_start(
                        wr_f[:].rearrange("p (c e) -> p c e", c=DC),
                        wr[:].rearrange("c p e -> p c e")), "ld", 16)
                    I(sync.dma_start(bb_f[:], bbs[:]), "ld", 16)
                    I(sync.dma_start(br_f[:], brs[:]), "ld", 16)

                @blk.vector
                def _(vector):
                    vector.memset(eps11[:], EPS)
                    W(vector, "ld", n["ld"])
                    vector.tensor_copy(wr_b[:], wr_f[:])
                    vector.tensor_copy(bb_b[:], bb_f[:])
                    vector.tensor_copy(br_b[:], br_f[:])

        def load_cast_resident(dst_sb, src_dram, nchunks=DC, width=1024):
            with ExitStack() as ph:
                stg = [ph.enter_context(
                    nc.sbuf_tensor(_nm("lcr"), [128, width], F32)) for _ in range(2)]
                b = dict(n)
                with nc.Block() as blk:
                    @blk.sync
                    def _(sync):
                        for dc in range(nchunks):
                            if dc >= 2:
                                W(sync, "ac", b["ac"] + dc - 1)
                            I(sync.dma_start(stg[dc % 2][:], src_dram[dc]), "ld", 16)

                    @blk.scalar
                    def _(scalar):
                        for dc in range(nchunks):
                            W(scalar, "ld", b["ld"] + 16 * (dc + 1))
                            I(scalar.copy(
                                dst_sb[:, dc * width:(dc + 1) * width],
                                stg[dc % 2][:]), "ac")


        def _dump_and_stop(pairs, sbuf_pairs=()):
            bq = dict(n)
            with nc.Block() as blk:
                @blk.sync
                def _(sync):
                    W(sync, "st", n["st"])
                    k = 0
                    for dst, srcap in pairs:
                        I(sync.dma_start(dst[:], srcap[:]), "st", 16)
                        k += 1
                    for dst, sb in sbuf_pairs:
                        I(sync.dma_start(dst[:], sb[:]), "st", 16)
                        k += 1
                    W(sync, "st", bq["st"] + 16 * k)

        with ExitStack() as ph:
            xo_b = ph.enter_context(nc.sbuf_tensor("xo_b", [128, DC * 1024], BF16))
            load_cast_resident(xo_b, xown)
            w_stg = ph.enter_context(nc.sbuf_tensor(_nm("ws"), [128, DC * 128], F32))
            w_bb = [ph.enter_context(nc.sbuf_tensor(_nm("wb"), [128, DC * 256], BF16))
                    for _ in range(2)]
            q_stg = [ph.enter_context(nc.sbuf_tensor(_nm("qs"), [128, 1024], BF16))
                     for _ in range(2)]
            q2s = [ph.enter_context(nc.sbuf_tensor(_nm("q2s"), [128, 1024], BF16))
                   for _ in range(2)]
            pq = [ph.enter_context(nc.psum_tensor(_nm("pq"), [128, 1024], F32))
                  for _ in range(4)]
            b = dict(n)
            with nc.Block() as blk:
                @blk.sync
                def _(sync):
                    for qg in range(16):
                        for hf in range(2):
                            W(sync, "ev", b["ev"] + 2 * qg + hf)
                            I(sync.dma_start(
                                w_stg[:].rearrange("p (c f) -> p c f", c=16),
                                wq[qg, 16 * hf:16 * (hf + 1)]
                                .rearrange("c p f -> p c f")), "ld", 16)

                @blk.vector
                def _(vector):
                    def qred(qg2):
                        par2 = 2 * (qg2 % 2)
                        for o in range(2):
                            W(vector, "qB", b["qB"] + 2 * qg2 + o + 1)
                            I(vector.reduce_max(
                                qmax_sb[0:1, 2 * qg2 + o:2 * qg2 + o + 1],
                                pq[par2 + o][0:1, 0:1024], AX.X), "qC")

                    for qg in range(16):
                        for hf in range(2):
                            W(vector, "ld", b["ld"] + 16 * (2 * qg + hf + 1))
                            if qg >= 2 and hf == 0:
                                W(vector, "mm", b["mm"] + qg - 1)
                            I(vector.tensor_copy(
                                w_bb[qg % 2][:, hf * 4096:(hf + 1) * 4096],
                                w_stg[:]), "ev")
                        if qg >= 2:
                            qred(qg - 2)
                    qred(14)
                    qred(15)

                @blk.tensor
                def _(tensor):
                    def qss_mms(qg2):
                        par2 = 2 * (qg2 % 2)
                        for o in range(2):
                            W(tensor, "qA", b["qA"] + 2 * qg2 + o + 1)
                            ins2 = None
                            for h in range(2):
                                ins2 = tensor.matmul(
                                    pq[par2 + o][0:1, h * 512:(h + 1) * 512],
                                    c_oc[:], q2s[o][:, h * 512:(h + 1) * 512],
                                    start=True, stop=True)
                            I(ins2, "qB")

                    for qg in range(16):
                        W(tensor, "ev", b["ev"] + 2 * (qg + 1))
                        if qg >= 2:
                            W(tensor, "ac", b["ac"] + 2 * (qg - 1))
                            qss_mms(qg - 2)
                            W(tensor, "qC", b["qC"] + 2 * (qg - 1))
                        par = 2 * (qg % 2)
                        ins = None
                        for dc in range(DC):
                            for o in range(2):
                                for h in range(2):
                                    ins = tensor.matmul(
                                        pq[par + o][:, h * 512:(h + 1) * 512],
                                        w_bb[qg % 2][:, dc * 256 + o * 128:dc * 256 + (o + 1) * 128],
                                        xo_b[:, dc * 1024 + h * 512:dc * 1024 + (h + 1) * 512],
                                        start=(dc == 0), stop=(dc == DC - 1))
                        I(ins, "mm")
                    W(tensor, "ac", b["ac"] + 32)
                    qss_mms(14)
                    qss_mms(15)

                @blk.scalar
                def _(scalar):
                    for qg in range(16):
                        W(scalar, "mm", b["mm"] + qg + 1)
                        par = 2 * (qg % 2)
                        for o in range(2):
                            W(scalar, "st", b["st"] + 16 * (2 * qg + o - 1))
                            I(scalar.activation(q_stg[o][:], pq[par + o][:],
                                                AF.Copy, scale=SCALE), "ac")
                        for o in range(2):
                            if qg >= 1:
                                W(scalar, "qB", b["qB"] + 2 * (qg - 1) + o + 1)
                            I(scalar.activation(q2s[o][:], q_stg[o][:],
                                                AF.Square), "qA")

                @blk.gpsimd
                def _(gpsimd):
                    for qg in range(16):
                        for o in range(2):
                            W(gpsimd, "ac", b["ac"] + 2 * qg + o + 1)
                            I(gpsimd.dma_start(
                                qdr[(2 * qg + o) * 128:(2 * qg + o + 1) * 128, :],
                                q_stg[o][:]), "st", 16)

        if stop_after == "q":
            _dump_and_stop([(dbg["dq"], qdr)])
            return nc

        with ExitStack() as ph:
            wk_bb = ph.enter_context(nc.sbuf_tensor("wk_bb", [128, DC * 1024], BF16))
            load_cast_resident(wk_bb, wk)
            xs = [ph.enter_context(nc.sbuf_tensor(_nm("x2s"), [128, 512], F32))
                  for _ in range(4)]
            xb = [ph.enter_context(nc.sbuf_tensor(_nm("x2b"), [128, 512], BF16))
                  for _ in range(4)]
            k_stg = [ph.enter_context(nc.sbuf_tensor(_nm("ks"), [128, 512], BF16))
                     for _ in range(2)]
            k2stg = [ph.enter_context(nc.sbuf_tensor(_nm("k2s"), [128, 512], BF16))
                     for _ in range(2)]
            kst = ph.enter_context(nc.sbuf_tensor(_nm("kst"), [1, 512], F32))
            kt11 = ph.enter_context(nc.sbuf_tensor(_nm("kt11"), [1, 1], F32))
            pk = [ph.enter_context(nc.psum_tensor(_nm("pk"), [128, 512], F32))
                  for _ in range(8)]
            b = dict(n)
            with nc.Block() as blk:
                @blk.sync
                def _(sync):
                    for ts in range(8):
                        for dc in range(DC):
                            idx = ts * DC + dc
                            if idx >= 4:
                                W(sync, "ac", b["ac"] + idx - 3)
                            I(sync.dma_start(
                                xs[idx % 4][:],
                                xfm[dc][:, ts * 512:(ts + 1) * 512]), "ld", 16)

                @blk.scalar
                def _(scalar):
                    for ts in range(8):
                        for dc in range(DC):
                            idx = ts * DC + dc
                            W(scalar, "ld", b["ld"] + 16 * (idx + 1))
                            if idx >= 4:
                                W(scalar, "mm", b["mm"] + idx - 3)
                            I(scalar.copy(xb[idx % 4][:], xs[idx % 4][:]), "ac")
                        for dkc in range(8):
                            j = ts * 8 + dkc
                            W(scalar, "ev", b["ev"] + j + 1)
                            if j >= 2:
                                W(scalar, "kA", b["kA"] + j - 1)
                            I(scalar.activation(k2stg[j % 2][:], k_stg[j % 2][:],
                                                AF.Square), "kA")

                @blk.tensor
                def _(tensor):
                    for ts in range(8):
                        for dc in range(DC):
                            idx = ts * DC + dc
                            W(tensor, "ac", b["ac"] + idx + 1)
                            if dc == 0 and ts >= 1:
                                W(tensor, "kB", b["kB"] + 8 * ts)
                            ins = None
                            for dkc in range(8):
                                ins = tensor.matmul(
                                    pk[dkc][:],
                                    wk_bb[:, dc * 1024 + dkc * 128:dc * 1024 + (dkc + 1) * 128],
                                    xb[idx % 4][:],
                                    start=(dc == 0), stop=(dc == DC - 1))
                            I(ins, "mm")
                        for dkc in range(8):
                            j = ts * 8 + dkc
                            W(tensor, "kA", b["kA"] + j + 1)
                            I(tensor.matmul(pk[dkc][0:1, 0:512], c_oc[:],
                                            k2stg[j % 2][:],
                                            start=True, stop=True), "ssc")

                @blk.vector
                def _(vector):
                    for ts in range(8):
                        W(vector, "mm", b["mm"] + (ts + 1) * DC)
                        for dkc in range(8):
                            j = ts * 8 + dkc
                            if j >= 2:
                                W(vector, "st", b["st"] + 16 * (j - 1))
                                W(vector, "kA", b["kA"] + j - 1)
                            I(vector.tensor_copy(k_stg[j % 2][:], pk[dkc][:]), "ev")
                        for dkc in range(8):
                            j = ts * 8 + dkc
                            W(vector, "ssc", b["ssc"] + j + 1)
                            vector.reduce_max(kt11[:], pk[dkc][0:1, 0:512], AX.X)
                            if ts == 0:
                                I(vector.tensor_copy(
                                    kmax_sb[0:1, dkc:dkc + 1], kt11[:]), "kB")
                            else:
                                I(vector.tensor_max(
                                    kmax_sb[0:1, dkc:dkc + 1],
                                    kmax_sb[0:1, dkc:dkc + 1], kt11[:]), "kB")

                @blk.gpsimd
                def _(gpsimd):
                    for ts in range(8):
                        for dkc in range(8):
                            j = ts * 8 + dkc
                            W(gpsimd, "ev", b["ev"] + j + 1)
                            I(gpsimd.dma_start(
                                kdr[dkc * 128:(dkc + 1) * 128, ts * 512:(ts + 1) * 512],
                                k_stg[j % 2][:]), "st", 16)

        ph35 = top.enter_context(ExitStack())
        v_sb = ph35.enter_context(nc.sbuf_tensor("v_sb", [128, DC * 1024], BF16))
        msk = ph35.enter_context(nc.sbuf_tensor("msk", [128, DC * 128], BF16))
        with ExitStack() as ph:
            wv_bb = ph.enter_context(nc.sbuf_tensor("wv_bb", [128, DC * 1024], BF16))
            load_cast_resident(wv_bb, wv)
            xs = [ph.enter_context(nc.sbuf_tensor(_nm("x3s"), [128, 512], F32))
                  for _ in range(4)]
            xb = [ph.enter_context(nc.sbuf_tensor(_nm("x3b"), [128, 512], BF16))
                  for _ in range(4)]
            pv = [ph.enter_context(nc.psum_tensor(_nm("pv"), [128, 1024], F32))
                  for _ in range(4)]
            b = dict(n)
            with nc.Block() as blk:
                @blk.sync
                def _(sync):
                    for ts in range(8):
                        for dc in range(DC):
                            idx = ts * DC + dc
                            if idx >= 4:
                                W(sync, "ac", b["ac"] + idx - 3)
                            I(sync.dma_start(
                                xs[idx % 4][:],
                                xfm[dc][:, ts * 512:(ts + 1) * 512]), "ld", 16)

                @blk.scalar
                def _(scalar):
                    for ts in range(8):
                        for dc in range(DC):
                            idx = ts * DC + dc
                            W(scalar, "ld", b["ld"] + 16 * (idx + 1))
                            if idx >= 4:
                                W(scalar, "mm", b["mm"] + idx - 3)
                            I(scalar.copy(xb[idx % 4][:], xs[idx % 4][:]), "ac")

                @blk.tensor
                def _(tensor):
                    for ts in range(8):
                        for dc in range(DC):
                            idx = ts * DC + dc
                            W(tensor, "ac", b["ac"] + idx + 1)
                            if dc == 0 and ts >= 1:
                                W(tensor, "ev", b["ev"] + 4 * ts)
                            ins = None
                            for tcl in range(4):
                                for h in range(2):
                                    ins = tensor.matmul(
                                        pv[tcl][:, h * 512:(h + 1) * 512],
                                        xb[idx % 4][:, tcl * 128:(tcl + 1) * 128],
                                        wv_bb[:, dc * 1024 + h * 512:dc * 1024 + (h + 1) * 512],
                                        start=(dc == 0), stop=(dc == DC - 1))
                            I(ins, "mm")

                @blk.vector
                def _(vector):
                    for ts in range(8):
                        W(vector, "mm", b["mm"] + (ts + 1) * DC)
                        for tcl in range(4):
                            tc = ts * 4 + tcl
                            I(vector.tensor_copy(
                                v_sb[:, tc * 1024:(tc + 1) * 1024], pv[tcl][:]), "ev")

        if stop_after == "kv":
            _dump_and_stop([(dbg["dq"], qdr), (dbg["dk"], kdr)],
                           [(dbg["dv"], v_sb)])
            return nc

        with ExitStack() as ph:
            K_sb = [ph.enter_context(nc.sbuf_tensor(_nm("Ksb"), [128, S], BF16))
                    for _ in range(2)]
            Q_sb = [ph.enter_context(nc.sbuf_tensor(_nm("Qsb"), [128, T], BF16))
                    for _ in range(2)]
            kmrep = ph.enter_context(nc.sbuf_tensor("kmrep", [1, 32], F32))
            prdrow = ph.enter_context(nc.sbuf_tensor("prdrow", [1, 32], F32))
            sqrow = ph.enter_context(nc.sbuf_tensor("sqrow", [1, 32], F32))
            ngbrow = ph.enter_context(nc.sbuf_tensor("ngbrow", [1, 32], F32))
            negb_all = ph.enter_context(nc.sbuf_tensor("negb_all", [128, 32], F32))
            r_sb = ph.enter_context(nc.sbuf_tensor("r_sb", [1, T], F32))
            R_sb = ph.enter_context(nc.sbuf_tensor("R_sb", [128, T], F32))
            pU = [ph.enter_context(nc.sbuf_tensor(_nm("pU"), [128, T], BF16))
                  for _ in range(2)]
            av_sb = [ph.enter_context(nc.sbuf_tensor(_nm("av"), [128, T], BF16))
                     for _ in range(2)]
            psc = [ph.enter_context(nc.psum_tensor(_nm("psc"), [128, 1024], F32))
                   for _ in range(2)]
            paux = ph.enter_context(nc.psum_tensor("paux", [1, 1024], F32))
            pav = ph.enter_context(nc.psum_tensor("pav", [128, 1024], F32))
            b = dict(n)

            def pieces(kc):
                qlo = 128 * (kc // 4)
                fd = 1024 - qlo
                offs = []
                off = 0
                while off < fd:
                    w_ = min(512, fd - off)
                    offs.append((off, w_))
                    off += w_
                return qlo, fd, offs

            with nc.Block() as blk:
                @blk.sync
                def _(sync):
                    W(sync, "st", n["st"])
                    I(sync.dma_start(
                        msk[:].rearrange("p (c k) -> p c k", c=DC),
                        maskstk[:].rearrange("c p k -> p c k")), "ld", 16)
                    for kvh in range(NKV):
                        if kvh >= 2:
                            W(sync, "ssc", b["ssc"] + 128 * (kvh - 1))
                        I(sync.dma_start(K_sb[kvh % 2][:],
                                         kdr[kvh * 128:(kvh + 1) * 128, :]), "ld", 16)
                        for qr in range(4):
                            it = kvh * 4 + qr
                            if it >= 2:
                                W(sync, "ssc", b["ssc"] + 32 * (it - 1))
                            I(sync.dma_start(Q_sb[it % 2][:],
                                             qdr[it * 128:(it + 1) * 128, :]),
                              "ld", 16)

                @blk.gpsimd
                def _(gpsimd):
                    kv3 = kmrep[:].rearrange("o (k r) -> o k r", r=4)
                    ins = None
                    for j in range(4):
                        ins = gpsimd.tensor_copy(kv3[:, :, j], kmax_sb[:])
                    I(ins, "po")
                    for it in range(32):
                        W(gpsimd, "rC", b["rC"] + it + 1)
                        I(gpsimd.dma_start(audr[it * 128:(it + 1) * 128, :],
                                           av_sb[it % 2][:]), "st", 16)

                @blk.vector
                def _(vector):
                    W(vector, "po", b["po"] + 1)
                    I(vector.tensor_mul(prdrow[:], qmax_sb[:], kmrep[:]), "qA")
                    W(vector, "qB", b["qB"] + 1)
                    I(vector.tensor_scalar_mul(ngbrow[:], sqrow[:], -1.0), "qC")
                    for kvh in range(NKV):
                        for qr in range(4):
                            it = kvh * 4 + qr
                            W(vector, "scn", b["scn"] + 32 * it + 32)
                            W(vector, "mm", b["mm"] + 2 * it + 1)
                            I(vector.reciprocal(r_sb[:], paux[0:1, 0:1024]), "rD")
                            W(vector, "qD", b["qD"] + 3 + 2 * it)
                            if it >= 2:
                                W(vector, "st", b["st"] + 16 * (it - 1))
                            I(vector.tensor_mul(av_sb[it % 2][:], pav[:], R_sb[:]),
                              "rC")

                @blk.scalar
                def _(scalar):
                    W(scalar, "qA", b["qA"] + 1)
                    I(scalar.activation(sqrow[:], prdrow[:], AF.Sqrt), "qB")
                    W(scalar, "mm", b["mm"] + 1)
                    I(scalar.copy(negb_all[:], psc[0][:, 0:32]), "qD")
                    for kvh in range(NKV):
                        for qr in range(4):
                            it = kvh * 4 + qr
                            for kc in range(DC):
                                qlo, fd, offs = pieces(kc)
                                W(scalar, "ssc", b["ssc"] + 32 * it + kc + 1)
                                if kc >= 2:
                                    W(scalar, "scn", b["scn"] + 32 * it + kc - 1)
                                I(scalar.activation(
                                    pU[kc % 2][:, 0:fd], psc[kc % 2][:, 0:fd],
                                    AF.Exp, bias=negb_all[:, it:it + 1]), "spr")
                            W(scalar, "rC", b["rC"] + it)
                            for h in range(2):
                                W(scalar, "mm", b["mm"] + 2 + 2 * it + h)
                                I(scalar.copy(R_sb[:, h * 512:(h + 1) * 512],
                                              psc[1][:, h * 512:(h + 1) * 512]),
                                  "qD")

                @blk.tensor
                def _(tensor):
                    W(tensor, "qC", b["qC"] + 1)
                    I(tensor.matmul(psc[0][:, 0:32], c_1f[:], ngbrow[:],
                                    start=True, stop=True), "mm")
                    for kvh in range(NKV):
                        W(tensor, "ld", b["ld"] + 16 * (5 * kvh + 2))
                        for qr in range(4):
                            it = kvh * 4 + qr
                            W(tensor, "ld", b["ld"] + 16 * (5 * kvh + 3 + qr))
                            def emit_scores(kc):
                                qlo, fd, offs = pieces(kc)
                                pp = psc[kc % 2]
                                W(tensor, "spr", b["spr"] + 32 * it + kc - 1)
                                if kc == 0 and it == 0:
                                    W(tensor, "qD", b["qD"] + 1)
                                if kc == 1:
                                    W(tensor, "qD", b["qD"] + 2 * it + 1)
                                for (off, w_) in offs:
                                    tensor.matmul(
                                        pp[:, off:off + w_],
                                        K_sb[kvh % 2][:, kc * 128:(kc + 1) * 128],
                                        Q_sb[it % 2][:, qlo + off:qlo + off + w_],
                                        start=True, stop=False)
                                I(tensor.matmul(
                                    pp[:, 0:128], c_id[:],
                                    msk[:, kc * 128:(kc + 1) * 128],
                                    start=False, stop=True), "ssc")

                            emit_scores(0)
                            for kc in range(DC):
                                qlo, fd, offs = pieces(kc)
                                if kc + 1 < DC:
                                    emit_scores(kc + 1)
                                W(tensor, "spr", b["spr"] + 32 * it + kc + 1)
                                if kc == 0:
                                    W(tensor, "rD", b["rD"] + it)
                                    W(tensor, "rC", b["rC"] + it)
                                for (off, w_) in offs:
                                    tensor.matmul(
                                        paux[0:1, qlo + off:qlo + off + w_],
                                        c_oc[:], pU[kc % 2][:, off:off + w_],
                                        start=(kc == 0), stop=(kc == DC - 1))
                                ins = None
                                for (off, w_) in offs:
                                    ins = tensor.matmul(
                                        pav[:, qlo + off:qlo + off + w_],
                                        v_sb[:, kc * 1024 + kvh * 128:kc * 1024 + (kvh + 1) * 128],
                                        pU[kc % 2][:, off:off + w_],
                                        start=(kc == 0), stop=(kc == DC - 1))
                                I(ins, "scn")
                            W(tensor, "rD", b["rD"] + it + 1)
                            for h in range(2):
                                I(tensor.matmul(
                                    psc[1][:, h * 512:(h + 1) * 512], c_1f[:],
                                    r_sb[0:1, h * 512:(h + 1) * 512],
                                    start=True, stop=True), "mm")

        if stop_after == "attn":
            _dump_and_stop([(dbg["dau"], audr)])
            return nc

        ph35.close()

        g_sb_ref = [None]

        def big_proj_phase(wsrc, rhs_sb, res_mode, res_dram, mdst, rdst, lora):
            with ExitStack() as ph:
                w_stg = ph.enter_context(
                    nc.sbuf_tensor(_nm("Pws"), [128, DC * 128], F32))
                w_bb = [ph.enter_context(
                    nc.sbuf_tensor(_nm("Pwb"), [128, DC * 256], BF16))
                    for _ in range(2)]
                x0s = ([ph.enter_context(
                    nc.sbuf_tensor(_nm("Px0"), [128, 1024], F32))
                    for _ in range(2)] if res_mode == "dram" else None)
                xp = [ph.enter_context(
                    nc.sbuf_tensor(_nm("Pxp"), [128, 1024], BF16))
                    for _ in range(2)]
                sq_b = ph.enter_context(nc.sbuf_tensor(_nm("Psq"), [128, 1024], BF16))
                fin = [ph.enter_context(nc.sbuf_tensor(_nm("Pf"), [1, 1024], F32))
                       for _ in range(6)]
                pwr = [ph.enter_context(nc.psum_tensor(_nm("Ppw"), [128, 1024], F32))
                       for _ in range(2)]
                pss = ph.enter_context(nc.psum_tensor(_nm("Pss"), [1, 1024], F32))
                psq = ph.enter_context(nc.psum_tensor(_nm("Psq"), [1, 1024], F32))
                b = dict(n)
                nload = 4 if res_mode == "dram" else 2
                with nc.Block() as blk:
                    @blk.sync
                    def _(sync):
                        W(sync, "st", n["st"])
                        for og in range(16):
                            for hf in range(2):
                                W(sync, "ev", b["ev"] + 2 * og + hf)
                                I(sync.dma_start(
                                    w_stg[:].rearrange("p (c f) -> p c f", c=16),
                                    wsrc[og, 16 * hf:16 * (hf + 1)]
                                    .rearrange("c p f -> p c f")), "ld", 16)
                            if res_mode == "dram":
                                for o in range(2):
                                    oc = 2 * og + o
                                    if oc >= 2:
                                        W(sync, "po", b["po"] + oc - 1)
                                    I(sync.dma_start(x0s[oc % 2][:], xown[oc]),
                                      "ld", 16)

                    @blk.vector
                    def _(vector):
                        for og in range(16):
                            for hf in range(2):
                                W(vector, "ld",
                                  b["ld"] + 16 * (nload * og + hf + 1))
                                if og >= 2 and hf == 0:
                                    W(vector, "mm", b["mm"] + 2 * og - 2)
                                I(vector.tensor_copy(
                                    w_bb[og % 2][:, hf * 4096:(hf + 1) * 4096],
                                    w_stg[:]), "ev")
                            for o in range(2):
                                oc = 2 * og + o
                                W(vector, "mm", b["mm"] + oc + 1)
                                if res_mode == "dram":
                                    W(vector, "ld",
                                      b["ld"] + 16 * (nload * og + 3 + o))
                                if oc >= 2:
                                    W(vector, "sA", b["sA"] + oc - 1)
                                    W(vector, "st", b["st"] + 16 * (oc - 1))
                                if res_mode == "dram":
                                    I(vector.tensor_add(xp[oc % 2][:],
                                                        pwr[oc % 2][:],
                                                        x0s[oc % 2][:]), "po")
                                else:
                                    I(vector.tensor_add(
                                        xp[oc % 2][:], pwr[oc % 2][:],
                                        rhs_sb[:, oc * 1024:(oc + 1) * 1024]),
                                      "po")
                        W(vector, "sA", b["sA"] + 32)
                        I(vector.tensor_scalar_mul(fin[0][:], pss[:],
                                                   1.0 / D), "ev")
                        I(vector.tensor_scalar_mul(fin[1][:], psq[:],
                                                   1.0 / D), "ev")
                        W(vector, "ac", b["ac"] + 33)
                        I(vector.tensor_sub(fin[3][:], fin[1][:], fin[2][:]), "ev")
                        W(vector, "ac", b["ac"] + 34)
                        I(vector.reciprocal(fin[5][:], fin[4][:]), "ev")

                    @blk.tensor
                    def _(tensor):
                        def stats_mms(oc2):
                            W(tensor, "ac", b["ac"] + oc2 + 1)
                            for h in range(2):
                                tensor.matmul(
                                    pss[0:1, h * 512:(h + 1) * 512], c_oc[:],
                                    xp[oc2 % 2][:, h * 512:(h + 1) * 512],
                                    start=(oc2 == 0), stop=(oc2 == 31))
                            ins2 = None
                            for h in range(2):
                                ins2 = tensor.matmul(
                                    psq[0:1, h * 512:(h + 1) * 512], c_oc[:],
                                    sq_b[:, h * 512:(h + 1) * 512],
                                    start=(oc2 == 0), stop=(oc2 == 31))
                            I(ins2, "sA")

                        for oc in range(DC):
                            og, o = oc // 2, oc % 2
                            W(tensor, "ev", b["ev"] + 2 * (og + 1))
                            if oc >= 2:
                                W(tensor, "po", b["po"] + oc - 1)
                            pw = pwr[oc % 2]
                            ins = None
                            for dc in range(DC):
                                for h in range(2):
                                    ins = tensor.matmul(
                                        pw[:, h * 512:(h + 1) * 512],
                                        w_bb[og % 2][:, dc * 256 + o * 128:dc * 256 + (o + 1) * 128],
                                        rhs_sb[:, dc * 1024 + h * 512:dc * 1024 + (h + 1) * 512],
                                        start=(dc == 0),
                                        stop=(dc == DC - 1 and not lora))
                            if lora:
                                for h in range(2):
                                    tensor.matmul(
                                        pw[:, h * 512:(h + 1) * 512],
                                        lb_b[:, oc * 128:(oc + 1) * 128],
                                        g_sb_ref[0][:, h * 512:(h + 1) * 512],
                                        start=False, stop=False)
                                for h in range(2):
                                    ins = tensor.matmul(
                                        pw[:, h * 512:(h + 1) * 512],
                                        bb_b[0:1, oc * 128:(oc + 1) * 128],
                                        c_or[:],
                                        start=False, stop=True)
                            I(ins, "mm")
                            if oc >= 2:
                                stats_mms(oc - 2)
                        stats_mms(30)
                        stats_mms(31)
                        W(tensor, "po", b["po"] + 32)
                        W(tensor, "ev", b["ev"] + 36)
                        for h in range(2):
                            tensor.matmul(pwr[0][:, h * 512:(h + 1) * 512],
                                          c_1f[:],
                                          fin[0][0:1, h * 512:(h + 1) * 512],
                                          start=True, stop=True)
                        ins = None
                        for h in range(2):
                            ins = tensor.matmul(pwr[1][:, h * 512:(h + 1) * 512],
                                                c_1f[:],
                                                fin[5][0:1, h * 512:(h + 1) * 512],
                                                start=True, stop=True)
                        I(ins, "mm")

                    @blk.scalar
                    def _(scalar):
                        for oc in range(DC):
                            W(scalar, "po", b["po"] + oc + 1)
                            W(scalar, "sA", b["sA"] + oc)
                            I(scalar.activation(sq_b[:], xp[oc % 2][:],
                                                AF.Square), "ac")
                        W(scalar, "ev", b["ev"] + 33)
                        I(scalar.activation(fin[2][:], fin[0][:], AF.Square), "ac")
                        W(scalar, "ev", b["ev"] + 35)
                        I(scalar.activation(fin[4][:], fin[3][:], AF.Sqrt,
                                            bias=eps11[:]), "ac")
                        W(scalar, "mm", b["mm"] + 33)
                        I(scalar.copy(mdst[:], pwr[0][:]), "ac")
                        I(scalar.copy(rdst[:], pwr[1][:]), "ac")

                    @blk.gpsimd
                    def _(gpsimd):
                        for oc in range(DC):
                            W(gpsimd, "po", b["po"] + oc + 1)
                            I(gpsimd.dma_start(res_dram[oc], xp[oc % 2][:]),
                              "st", 16)

        with ExitStack() as ph6:
            au_sb = ph6.enter_context(nc.sbuf_tensor("au_sb", [128, DC * 1024],
                                                     BF16))
            b6 = dict(n)
            with nc.Block() as blk:
                @blk.sync
                def _(sync):
                    W(sync, "st", n["st"])
                    I(sync.dma_start(
                        au_sb[:].rearrange("p (c t) -> p c t", c=DC),
                        audr[:].rearrange("(c p) t -> p c t", p=128)), "ld", 16)

                @blk.tensor
                def _(tensor):
                    W(tensor, "ld", b6["ld"] + 16)
            big_proj_phase(wo, au_sb, "dram", x1dr, m1, r1, lora=False)

        ph78 = top.enter_context(ExitStack())
        m2 = ph78.enter_context(nc.sbuf_tensor("m2", [128, T], F32))
        r2 = ph78.enter_context(nc.sbuf_tensor("r2", [128, T], F32))
        with ExitStack() as ph7:
            la_b = ph7.enter_context(nc.sbuf_tensor("la_b", [128, DC * 128], BF16))
            lb_b = ph7.enter_context(nc.sbuf_tensor("lb_b", [128, DC * 128], BF16))
            with ExitStack() as ph:
                la_f = ph.enter_context(nc.sbuf_tensor("la_f", [128, DC * 128],
                                                       F32))
                bl = dict(n)
                with nc.Block() as blk:
                    @blk.sync
                    def _(sync):
                        I(sync.dma_start(
                            la_f[:].rearrange("p (c f) -> p c f", c=DC),
                            la[:].rearrange("c p f -> p c f")), "ld", 16)
                        W(sync, "ld", bl["ld"] + 16)

                    @blk.vector
                    def _(vector):
                        W(vector, "ld", bl["ld"] + 16)
                        vector.tensor_copy(la_b[:], la_f[:])
                with nc.Block() as blk:
                    @blk.sync
                    def _(sync):
                        I(sync.dma_start(
                            la_f[:].rearrange("p (c f) -> p c f", c=DC),
                            lb[:].rearrange("c f p -> f c p")), "ld", 16)
                        W(sync, "ld", bl["ld"] + 32)

                    @blk.vector
                    def _(vector):
                        W(vector, "ld", bl["ld"] + 32)
                        vector.tensor_copy(lb_b[:], la_f[:])
            x1_sb = ph7.enter_context(nc.sbuf_tensor("x1_sb", [128, DC * 1024],
                                                     BF16))
            with ExitStack() as ph:
                xps = [ph.enter_context(nc.sbuf_tensor(_nm("l1a"), [128, 1024],
                                                       BF16)) for _ in range(2)]
                tmp = [ph.enter_context(nc.sbuf_tensor(_nm("l1t"), [128, 1024],
                                                       F32)) for _ in range(2)]
                b = dict(n)
                with nc.Block() as blk:
                    @blk.sync
                    def _(sync):
                        W(sync, "st", n["st"])
                        for oc in range(DC):
                            if oc >= 2:
                                W(sync, "ev", b["ev"] + oc - 1)
                            I(sync.dma_start(xps[oc % 2][:], x1dr[oc]), "ld", 16)

                    @blk.vector
                    def _(vector):
                        for oc in range(DC):
                            W(vector, "ld", b["ld"] + 16 * (oc + 1))
                            if oc >= 2:
                                W(vector, "po", b["po"] + oc - 1)
                            I(vector.tensor_sub(tmp[oc % 2][:], xps[oc % 2][:],
                                                m1[:]), "ev")

                    @blk.gpsimd
                    def _(gpsimd):
                        for oc in range(DC):
                            W(gpsimd, "ev", b["ev"] + oc + 1)
                            I(gpsimd.tensor_mul(
                                x1_sb[:, oc * 1024:(oc + 1) * 1024],
                                tmp[oc % 2][:], r1[:]), "po")

            if stop_after == "x1":
                _dump_and_stop([], [(dbg["dx1"], x1_sb)])
                return nc

            g_sb = ph7.enter_context(nc.sbuf_tensor("g_sb", [128, T], BF16))
            g_sb_ref[0] = g_sb
            with ExitStack() as ph:
                z_sb = ph.enter_context(nc.sbuf_tensor("z_sb", [128, 64], F32))
                zs = ph.enter_context(nc.sbuf_tensor("zs", [128, 64], F32))
                cum = ph.enter_context(nc.sbuf_tensor("cum", [128, 64], F32))
                tmp6 = ph.enter_context(nc.sbuf_tensor("tmp6", [128, 64], F32))
                ind = ph.enter_context(nc.sbuf_tensor("ind", [128, 64], F32))
                kz8 = ph.enter_context(nc.sbuf_tensor("kz8", [128, 8], F32))
                ts8 = ph.enter_context(nc.sbuf_tensor("ts8", [128, 8], F32))
                tau8 = ph.enter_context(nc.sbuf_tensor("tau8", [128, 8], F32))
                kzr = ph.enter_context(nc.sbuf_tensor("kzr", [128, 8], F32))
                trep = ph.enter_context(nc.sbuf_tensor("trep", [128, 64], F32))
                w_f = ph.enter_context(nc.sbuf_tensor("w_f", [128, 64], F32))
                w_b2 = ph.enter_context(nc.sbuf_tensor("w_b2", [128, 64], BF16))
                weT = ph.enter_context(nc.sbuf_tensor("weT", [E, T], BF16))
                wer = ph.enter_context(nc.sbuf_tensor("wer", [128, T], BF16))
                plg = ph.enter_context(nc.psum_tensor("plg", [128, E], F32))
                ptr8 = ph.enter_context(nc.psum_tensor("ptr8", [E, 128], BF16))
                pwer = ph.enter_context(nc.psum_tensor("pwer", [128, 1024], F32))
                phh = ph.enter_context(nc.psum_tensor("phh", [128, 1024], F32))

                b = dict(n)
                with nc.Block() as blk:
                    @blk.tensor
                    def _(tensor):
                        for tc in range(8):
                            if tc >= 1:
                                W(tensor, "ac", b["ac"] + tc)
                            for dc in range(DC):
                                tensor.matmul(
                                    plg[:],
                                    x1_sb[:, dc * 1024 + tc * 128:dc * 1024 + (tc + 1) * 128],
                                    wr_b[:, dc * E:(dc + 1) * E],
                                    start=(dc == 0), stop=False)
                            I(tensor.matmul(plg[:], c_1b[:], br_b[:],
                                            start=False, stop=True), "mm")

                    @blk.scalar
                    def _(scalar):
                        for tc in range(8):
                            W(scalar, "mm", b["mm"] + tc + 1)
                            I(scalar.copy(z_sb[:, tc * 8:(tc + 1) * 8], plg[:]),
                              "ac")

                def v3(t_):
                    return t_[:].rearrange("p (g e) -> p g e", e=8)

                def v5(t_):
                    return t_[:].rearrange("p (g i j k) -> p g i j k",
                                           i=2, j=2, k=2)

                def sva(t_, off):
                    return t_[:].rearrange("p (g a e) -> p g a e",
                                           a=2, e=4)[:, :, :, off:off + 1]

                def s6v(t_, off):
                    return v3(t_)[:, :, off:off + 6:2]

                ops = []

                def add(fn, dve_only=True):
                    ops.append((fn, False))

                def addc(fn):
                    ops.append((fn, True))

                addc(lambda e: e.tensor_copy(zs[:], z_sb[:]))
                for (va, vb) in [
                    (lambda t: v5(t)[:, :, :, :, 0], lambda t: v5(t)[:, :, :, :, 1]),
                    (lambda t: v5(t)[:, :, :, 0, :], lambda t: v5(t)[:, :, :, 1, :]),
                    (lambda t: sva(t, 1), lambda t: sva(t, 2)),
                    (lambda t: v5(t)[:, :, 0, :, :], lambda t: v5(t)[:, :, 1, :, :]),
                    (lambda t: v3(t)[:, :, 2:4], lambda t: v3(t)[:, :, 4:6]),
                    (lambda t: s6v(t, 1), lambda t: s6v(t, 2)),
                ]:
                    add(lambda e, va=va, vb=vb: e.tensor_tensor(
                        va(tmp6), va(zs), vb(zs), ALU.max))
                    add(lambda e, va=va, vb=vb: e.tensor_tensor(
                        vb(tmp6), va(zs), vb(zs), ALU.min))
                    addc(lambda e, va=va, vb=vb: e.tensor_copy(va(zs), va(tmp6)))
                    addc(lambda e, va=va, vb=vb: e.tensor_copy(vb(zs), vb(tmp6)))
                addc(lambda e: e.tensor_copy(cum[:], zs[:]))
                for d_ in (1, 2, 4):
                    add(lambda e, d_=d_: e.tensor_tensor(
                        v3(tmp6)[:, :, d_:8], v3(cum)[:, :, d_:8],
                        v3(cum)[:, :, 0:8 - d_], ALU.add))
                    addc(lambda e, d_=d_: e.tensor_copy(
                        v3(cum)[:, :, d_:8], v3(tmp6)[:, :, d_:8]))
                add(lambda e: e.tensor_mul(tmp6[:], zs[:], c_kv[:]))
                add(lambda e: e.tensor_sub(tmp6[:], tmp6[:], cum[:]))
                add(lambda e: e.tensor_single_scalar(ind[:], tmp6[:], -1.0,
                                                     ALU.is_gt))
                add(lambda e: e.tensor_reduce(kz8[:], v3(ind), AX.X, ALU.add))
                addc(lambda e: e.tensor_copy(tmp6[:], ind[:]))
                add(lambda e: e.tensor_tensor(
                    v3(tmp6)[:, :, 0:7], v3(ind)[:, :, 0:7], v3(ind)[:, :, 1:8],
                    ALU.subtract))
                add(lambda e: e.tensor_mul(tmp6[:], tmp6[:], cum[:]))
                add(lambda e: e.tensor_reduce(ts8[:], v3(tmp6), AX.X, ALU.add))
                add(lambda e: e.tensor_single_scalar(ts8[:], ts8[:], 1.0,
                                                     ALU.subtract))
                add(lambda e: e.reciprocal(kzr[:], kz8[:]), dve_only=True)
                add(lambda e: e.tensor_mul(tau8[:], ts8[:], kzr[:]))
                addc(lambda e: e.tensor_copy(v3(trep)[:, :, 0], tau8[:]))
                addc(lambda e: e.tensor_copy(v3(trep)[:, :, 1], tau8[:]))
                addc(lambda e: e.tensor_copy(v3(trep)[:, :, 2:4],
                                            v3(trep)[:, :, 0:2]))
                addc(lambda e: e.tensor_copy(v3(trep)[:, :, 4:8],
                                            v3(trep)[:, :, 0:4]))
                add(lambda e: e.tensor_sub(w_f[:], z_sb[:], trep[:]))
                add(lambda e: e.tensor_scalar_max(w_f[:], w_f[:], 0.0))

                sched = []
                last = 1
                for (fn, pool_ok) in ops:
                    if pool_ok and last == 0:
                        sched.append((1, fn))
                    else:
                        if last == 0:
                            sched.append((1, None))
                        sched.append((0, fn))
                    last = sched[-1][0]
                bnc = ph.enter_context(nc.sbuf_tensor("bnc", [1, 4], F32))
                b = dict(n)
                nv = sum(1 for e_, _ in sched if e_ == 0)
                np_ = sum(1 for e_, _ in sched if e_ == 1)
                with nc.Block() as blk:
                    @blk.vector
                    def _(vector):
                        cv = cp = 0
                        for (e_, fn) in sched:
                            if e_ == 0:
                                W(vector, "po", b["po"] + cp)
                                fn_ = fn if fn else (
                                    lambda e: e.tensor_copy(bnc[0:1, 0:2],
                                                            bnc[0:1, 2:4]))
                                I(fn_(vector), "ev")
                                cv += 1
                            else:
                                cp += 1

                    @blk.gpsimd
                    def _(gpsimd):
                        cv = cp = 0
                        for (e_, fn) in sched:
                            if e_ == 1:
                                W(gpsimd, "ev", b["ev"] + cv)
                                fn_ = fn if fn else (
                                    lambda e: e.tensor_copy(bnc[0:1, 0:2],
                                                            bnc[0:1, 2:4]))
                                I(fn_(gpsimd), "po")
                                cp += 1
                            else:
                                cv += 1

                if stop_after == "w":
                    _dump_and_stop([], [(dbg["dz"], z_sb), (dbg["dw"], w_f)])
                    return nc

                b = dict(n)
                with nc.Block() as blk:
                    @blk.scalar
                    def _(scalar):
                        I(scalar.copy(w_b2[:], w_f[:]), "ac")
                        for tc in range(8):
                            W(scalar, "mm", b["mm"] + tc + 1)
                            I(scalar.copy(weT[0:E, tc * 128:(tc + 1) * 128],
                                          ptr8[:]), "ac")
                        W(scalar, "mm", b["mm"] + 9)
                        I(scalar.copy(wer[:], pwer[:]), "ac")

                    @blk.tensor
                    def _(tensor):
                        for tc in range(8):
                            W(tensor, "ac", b["ac"] + 1 + tc)
                            I(tensor.transpose(ptr8[:],
                                               w_b2[:, tc * 8:(tc + 1) * 8],
                                               c_id[:]), "mm")
                        W(tensor, "ac", b["ac"] + 9)
                        ins = None
                        for h in range(2):
                            ins = tensor.matmul(
                                pwer[:, h * 512:(h + 1) * 512], c_e8[:],
                                weT[0:E, h * 512:(h + 1) * 512],
                                start=True, stop=True)
                        I(ins, "mm")
                        W(tensor, "ac", b["ac"] + 10)
                        ins = None
                        for dc in range(DC):
                            for h in range(2):
                                ins = tensor.matmul(
                                    phh[:, h * 512:(h + 1) * 512],
                                    la_b[:, dc * 128:(dc + 1) * 128],
                                    x1_sb[:, dc * 1024 + h * 512:dc * 1024 + (h + 1) * 512],
                                    start=(dc == 0), stop=(dc == DC - 1))
                        I(ins, "mm")

                    @blk.vector
                    def _(vector):
                        W(vector, "mm", b["mm"] + 10)
                        I(vector.tensor_mul(g_sb[:], phh[:], wer[:]), "ev")

            if stop_after == "g":
                _dump_and_stop([], [(dbg["dz"], z_sb), (dbg["dw"], w_f),
                                    (dbg["dg"], g_sb)])
                return nc

            big_proj_phase(wb, x1_sb, "sbuf", x2dr, m2, r2, lora=True)

        if stop_after == "x2":
            _dump_and_stop([(dbg["dx2"], x2dr)],
                           [(dbg["dm2"], m2), (dbg["dr2"], r2)])
            return nc

        with ExitStack() as ph:
            xps = [ph.enter_context(nc.sbuf_tensor(_nm("l2a"), [128, 1024], BF16))
                   for _ in range(2)]
            tmp = [ph.enter_context(nc.sbuf_tensor(_nm("l2t"), [128, 1024], F32))
                   for _ in range(2)]
            oc_f = [ph.enter_context(nc.sbuf_tensor(_nm("l2o"), [128, 1024], F32))
                    for _ in range(2)]
            b = dict(n)
            with nc.Block() as blk:
                @blk.sync
                def _(sync):
                    W(sync, "st", n["st"])
                    for oc in range(DC):
                        if oc >= 2:
                            W(sync, "ev", b["ev"] + oc - 1)
                        I(sync.dma_start(xps[oc % 2][:], x2dr[oc]), "ld", 16)
                        if oc >= 2:
                            W(sync, "po", b["po"] + oc - 1)
                            I(sync.dma_start(
                                out[(oc - 2) * 128:(oc - 1) * 128, :],
                                oc_f[oc % 2][:]), "st", 16)
                    for oc in range(DC - 2, DC):
                        W(sync, "po", b["po"] + oc + 1)
                        I(sync.dma_start(out[oc * 128:(oc + 1) * 128, :],
                                         oc_f[oc % 2][:]), "st", 16)
                    W(sync, "st", b["st"] + 16 * DC)

                @blk.vector
                def _(vector):
                    for oc in range(DC):
                        W(vector, "ld", b["ld"] + 16 * (oc + 1))
                        if oc >= 2:
                            W(vector, "po", b["po"] + oc - 1)
                        I(vector.tensor_sub(tmp[oc % 2][:], xps[oc % 2][:], m2[:]),
                          "ev")

                @blk.gpsimd
                def _(gpsimd):
                    for oc in range(DC):
                        W(gpsimd, "ev", b["ev"] + oc + 1)
                        if oc >= 2:
                            W(gpsimd, "st", b["st"] + 16 * (oc - 1))
                        I(gpsimd.tensor_mul(oc_f[oc % 2][:], tmp[oc % 2][:],
                                            r2[:]), "po")

    return nc


def kernel(**inputs):
    in_maps, own = _stage_inputs(inputs)
    nc = build_program()
    res = run_bass_kernel_spmd(nc, in_maps, core_ids=list(range(NCORE)))
    outp = np.zeros((B, S, D), dtype=np.float32)
    for c in range(NCORE):
        b_, rows = own[c]
        o = res.results[c]["out"]
        outp[b_][rows] = o.T
    return outp


# version 17
# speedup vs baseline: 1.4928x; 1.0808x over previous
import numpy as np
import ml_dtypes
from contextlib import ExitStack

import concourse.bass as bass
import concourse.mybir as mybir
from concourse.bass_utils import run_bass_kernel_spmd

F32 = mybir.dt.float32
BF16 = mybir.dt.bfloat16
AF = mybir.ActivationFunctionType
ALU = mybir.AluOpType
AX = mybir.AxisListType
BF = ml_dtypes.bfloat16

B, S, D = 2, 4096, 4096
NKV = 8
E, R = 8, 16
NCORE = 8
T = 1024
DC = 32
SCALE = float(128) ** -0.5
NEG = -1.0e30
EPS = 1e-5


def _stage_inputs(inputs):
    x = np.asarray(inputs["L0_attn_hidden_states"], dtype=np.float32)
    Wq = np.asarray(inputs["L0_attn_Wq"], dtype=np.float32)
    Wk = np.asarray(inputs["L0_attn_Wk"], dtype=np.float32)
    Wv = np.asarray(inputs["L0_attn_Wv"], dtype=np.float32)
    Wo = np.asarray(inputs["L0_attn_Wo"], dtype=np.float32)
    Wb = np.asarray(inputs["L0_ffn_W_base"], dtype=np.float32)
    bb = np.asarray(inputs["L0_ffn_b_base"], dtype=np.float32)
    Wr = np.asarray(inputs["L0_ffn_W_router"], dtype=np.float32)
    br = np.asarray(inputs["L0_ffn_b_router"], dtype=np.float32)
    lA = np.asarray(inputs["L0_ffn_lora_A"], dtype=np.float32)
    lB = np.asarray(inputs["L0_ffn_lora_B"], dtype=np.float32)

    def qg_major(W_din_dout):
        w = W_din_dout.reshape(DC, 128, DC, 128)
        w = w.transpose(2, 0, 1, 3)
        w = w.reshape(16, 2, DC, 128, 128).transpose(0, 2, 3, 1, 4)
        return np.ascontiguousarray(w.reshape(16, DC, 128, 256))

    kvec = np.ascontiguousarray(np.broadcast_to(
        np.tile(np.arange(1, 9, dtype=np.float32), 8)[None, :], (128, 64)))

    shared = {
        "wq": qg_major(Wq),
        "wo": qg_major(Wo),
        "wb": qg_major(np.ascontiguousarray(Wb.T)),
        "wk": np.ascontiguousarray(Wk.reshape(DC, 128, 1024)),
        "wv": np.ascontiguousarray(Wv.reshape(DC, 128, 1024)),
        "wr": np.ascontiguousarray(Wr.T.reshape(DC, 128, E)),
        "la": np.ascontiguousarray(lA.transpose(1, 0, 2).reshape(DC, 128, E * R)),
        "lb": np.ascontiguousarray(lB.reshape(E * R, DC, 128).transpose(1, 0, 2)),
        "bbs": np.ascontiguousarray(bb.reshape(1, D)),
        "brs": np.ascontiguousarray(br.reshape(1, E)),
        "idm": np.eye(128, dtype=np.float32).astype(BF),
        "onescol": np.ones((128, 1), dtype=BF),
        "ones1f": np.ones((1, 128), dtype=np.float32),
        "ones1b": np.ones((1, 128), dtype=BF),
        "onesrow": np.ones((1, 512), dtype=BF),
        "e8m": np.kron(np.eye(E, dtype=np.float32),
                       np.ones((1, R), dtype=np.float32)).astype(BF),
        "kvecf": kvec,
    }

    xfm = [np.ascontiguousarray(x[bb_].T).reshape(DC, 128, S) for bb_ in range(B)]

    in_maps, own = [], []
    for c in range(NCORE):
        b_, i = c // 4, c % 4
        rows = np.concatenate([np.arange((4 * s + i) * 128, (4 * s + i + 1) * 128)
                               for s in range(8)])
        own.append((b_, rows))
        xo = np.ascontiguousarray(x[b_][rows].T).reshape(DC, 128, T)
        mk = np.zeros((DC, 128, 128), dtype=np.float32)
        for kc in range(DC):
            g = 4 * (kc // 4) + i
            if kc > g:
                mk[kc, :, :] = NEG
            elif kc == g:
                kk = np.arange(128)[:, None]
                qq = np.arange(128)[None, :]
                mk[kc] = np.where(kk <= qq, 0.0, NEG)
        m = dict(shared)
        m["xfm"] = xfm[b_]
        m["xown"] = xo
        m["maskstk"] = np.ascontiguousarray(mk.astype(BF))
        in_maps.append(m)
    return in_maps, own


def build_program(stop_after="all"):
    nc = bass.Bass("TRN2", target_bir_lowering=False)

    def din(name, shape, dt=F32):
        return nc.dram_tensor(name, shape, dt, kind="ExternalInput").ap()

    xfm = din("xfm", [DC, 128, S])
    xown = din("xown", [DC, 128, T])
    wq = din("wq", [16, DC, 128, 256])
    wo = din("wo", [16, DC, 128, 256])
    wb = din("wb", [16, DC, 128, 256])
    wk = din("wk", [DC, 128, 1024])
    wv = din("wv", [DC, 128, 1024])
    wr = din("wr", [DC, 128, E])
    la = din("la", [DC, 128, E * R])
    lb = din("lb", [DC, E * R, 128])
    bbs = din("bbs", [1, D])
    brs = din("brs", [1, E])
    idm = din("idm", [128, 128], BF16)
    onescol = din("onescol", [128, 1], BF16)
    ones1f = din("ones1f", [1, 128])
    ones1b = din("ones1b", [1, 128], BF16)
    onesrow = din("onesrow", [1, 512], BF16)
    e8m = din("e8m", [E, E * R], BF16)
    maskstk = din("maskstk", [DC, 128, 128], BF16)
    kvecf = din("kvecf", [128, 64])

    qdr = nc.dram_tensor("qdr", [D, T], BF16, kind="Internal").ap()
    kdr = nc.dram_tensor("kdr", [1024, S], BF16, kind="Internal").ap()
    audr = nc.dram_tensor("audr", [D, T], BF16, kind="Internal").ap()
    x1dr = nc.dram_tensor("x1dr", [DC, 128, T], BF16, kind="Internal").ap()
    x2dr = nc.dram_tensor("x2dr", [DC, 128, T], BF16, kind="Internal").ap()
    out = nc.dram_tensor("out", [D, T], F32, kind="ExternalOutput").ap()

    uid = [0]

    def _nm(p):
        uid[0] += 1
        return f"{p}{uid[0]}"

    dbg = {}
    if stop_after in ("q", "kv"):
        dbg["dq"] = nc.dram_tensor("dq", [D, T], BF16, kind="ExternalOutput").ap()
    if stop_after == "kv":
        dbg["dk"] = nc.dram_tensor("dk", [1024, S], BF16,
                                   kind="ExternalOutput").ap()
        dbg["dv"] = nc.dram_tensor("dv", [128, DC * 1024], BF16,
                                   kind="ExternalOutput").ap()
    if stop_after == "attn":
        dbg["dau"] = nc.dram_tensor("dau", [D, T], BF16,
                                    kind="ExternalOutput").ap()
    if stop_after == "x1":
        dbg["dx1"] = nc.dram_tensor("dx1", [128, DC * 1024], BF16,
                                    kind="ExternalOutput").ap()
    if stop_after in ("w", "g"):
        dbg["dz"] = nc.dram_tensor("dz", [128, 64], F32,
                                   kind="ExternalOutput").ap()
        dbg["dw"] = nc.dram_tensor("dw", [128, 64], F32,
                                   kind="ExternalOutput").ap()
    if stop_after == "g":
        dbg["dg"] = nc.dram_tensor("dg", [128, T], BF16,
                                   kind="ExternalOutput").ap()
    if stop_after == "x2":
        dbg["dx2"] = nc.dram_tensor("dx2", [DC, 128, T], BF16,
                                    kind="ExternalOutput").ap()
        dbg["dm2"] = nc.dram_tensor("dm2", [128, T], F32,
                                    kind="ExternalOutput").ap()
        dbg["dr2"] = nc.dram_tensor("dr2", [128, T], F32,
                                    kind="ExternalOutput").ap()

    with ExitStack() as top:
        c_id = top.enter_context(nc.sbuf_tensor("c_id", [128, 128], BF16))
        c_oc = top.enter_context(nc.sbuf_tensor("c_oc", [128, 1], BF16))
        c_1f = top.enter_context(nc.sbuf_tensor("c_1f", [1, 128], F32))
        c_1b = top.enter_context(nc.sbuf_tensor("c_1b", [1, 128], BF16))
        c_or = top.enter_context(nc.sbuf_tensor("c_or", [1, 512], BF16))
        c_e8 = top.enter_context(nc.sbuf_tensor("c_e8", [E, E * R], BF16))
        c_kv = top.enter_context(nc.sbuf_tensor("c_kv", [128, 64], F32))
        wr_b = top.enter_context(nc.sbuf_tensor("wr_b", [128, DC * E], BF16))
        bb_b = top.enter_context(nc.sbuf_tensor("bb_b", [1, D], BF16))
        br_b = top.enter_context(nc.sbuf_tensor("br_b", [1, E], BF16))
        m1 = top.enter_context(nc.sbuf_tensor("m1", [128, T], F32))
        r1 = top.enter_context(nc.sbuf_tensor("r1", [128, T], F32))
        eps11 = top.enter_context(nc.sbuf_tensor("eps11", [1, 1], F32))
        qmax_sb = top.enter_context(nc.sbuf_tensor("qmax_sb", [1, 32], F32))
        kmax_sb = top.enter_context(nc.sbuf_tensor("kmax_sb", [1, 8], F32))

        sems = {}
        for nm in ["ld", "st", "mm", "ac", "ev", "po",
                   "ssc", "spr", "scn", "kA", "kB", "kC",
                   "qA", "qB", "qC", "qD", "pA",
                   "rA", "rB", "rC", "rD", "sA", "lw", "cw", "lc"]:
            sems[nm] = top.enter_context(nc.semaphore(nm))
        n = {k: 0 for k in sems}

        def I(instr, nm, amt=1):
            instr.then_inc(sems[nm], amt)
            n[nm] += amt

        def W(eng, nm, val):
            if val > 0:
                eng.wait_ge(sems[nm], val)

        bigA = top.enter_context(nc.sbuf_tensor("bigA", [128, DC * 1024], BF16))
        bigB = top.enter_context(nc.sbuf_tensor("bigB", [128, DC * 1024], BF16))
        msk = bigA[:, 0:DC * 128].bitcast(BF16) if False else None
        msk = bigA[:, DC * 512:DC * 512 + DC * 128]
        xo_b = bigA
        wk_bb = bigB
        wv_bb = bigA
        v_sb = bigB
        with ExitStack() as ph:
            wr_f = ph.enter_context(nc.sbuf_tensor("wr_f", [128, DC * E], F32))
            bb_f = ph.enter_context(nc.sbuf_tensor("bb_f", [1, D], F32))
            br_f = ph.enter_context(nc.sbuf_tensor("br_f", [1, E], F32))
            with nc.Block() as blk:
                @blk.sync
                def _(sync):
                    I(sync.dma_start(c_id[:], idm[:]), "lc", 16)
                    I(sync.dma_start(c_oc[:], onescol[:]), "lc", 16)
                    I(sync.dma_start(c_1f[:], ones1f[:]), "lc", 16)
                    I(sync.dma_start(c_1b[:], ones1b[:]), "lc", 16)
                    I(sync.dma_start(c_or[:], onesrow[:]), "lc", 16)
                    I(sync.dma_start(c_e8[:], e8m[:]), "lc", 16)
                    I(sync.dma_start(c_kv[:], kvecf[:]), "lc", 16)
                    I(sync.dma_start(
                        wr_f[:].rearrange("p (c e) -> p c e", c=DC),
                        wr[:].rearrange("c p e -> p c e")), "lc", 16)
                    I(sync.dma_start(bb_f[:], bbs[:]), "lc", 16)
                    I(sync.dma_start(br_f[:], brs[:]), "lc", 16)

                @blk.gpsimd
                def _(gpsimd):
                    I(gpsimd.dma_start(
                        xo_b[:].rearrange("p (c f) -> p c f", c=DC),
                        xown[:].rearrange("c p f -> p c f")), "cw", 16)

                @blk.vector
                def _(vector):
                    vector.memset(eps11[:], EPS)
                    W(vector, "lc", n["lc"])
                    vector.tensor_copy(wr_b[:], wr_f[:])
                    vector.tensor_copy(bb_b[:], bb_f[:])
                    vector.tensor_copy(br_b[:], br_f[:])

        def load_cast_resident(dst_sb, src_dram, nchunks=DC, width=1024):
            with ExitStack() as ph:
                stg = [ph.enter_context(
                    nc.sbuf_tensor(_nm("lcr"), [128, width], F32)) for _ in range(2)]
                b = dict(n)
                with nc.Block() as blk:
                    @blk.sync
                    def _(sync):
                        for dc in range(nchunks):
                            if dc >= 2:
                                W(sync, "ac", b["ac"] + dc - 1)
                            I(sync.dma_start(stg[dc % 2][:], src_dram[dc]), "lc", 16)

                    @blk.scalar
                    def _(scalar):
                        for dc in range(nchunks):
                            W(scalar, "ld", b["ld"] + 16 * (dc + 1))
                            I(scalar.copy(
                                dst_sb[:, dc * width:(dc + 1) * width],
                                stg[dc % 2][:]), "ac")


        def _dump_and_stop(pairs, sbuf_pairs=()):
            bq = dict(n)
            with nc.Block() as blk:
                @blk.sync
                def _(sync):
                    W(sync, "st", n["st"])
                    k = 0
                    for dst, srcap in pairs:
                        I(sync.dma_start(dst[:], srcap[:]), "st", 16)
                        k += 1
                    for dst, sb in sbuf_pairs:
                        I(sync.dma_start(dst[:], sb[:]), "st", 16)
                        k += 1
                    W(sync, "st", bq["st"] + 16 * k)

        with ExitStack() as ph:
            w_bb = [ph.enter_context(nc.sbuf_tensor(_nm("wb"), [128, DC * 256], BF16))
                    for _ in range(2)]
            q_stg = [ph.enter_context(nc.sbuf_tensor(_nm("qs"), [128, 1024], BF16))
                     for _ in range(2)]
            q2s = [ph.enter_context(nc.sbuf_tensor(_nm("q2s"), [128, 1024], BF16))
                   for _ in range(2)]
            pq = [ph.enter_context(nc.psum_tensor(_nm("pq"), [128, 1024], F32))
                  for _ in range(4)]
            b = dict(n)
            with nc.Block() as blk:
                @blk.gpsimd
                def _(gpsimd):
                    for qg in range(16):
                        if qg >= 2:
                            W(gpsimd, "mm", b["mm"] + qg - 1)
                        I(gpsimd.dma_start(
                            w_bb[qg % 2][:].rearrange("p (c f) -> p c f", c=DC),
                            wq[qg].rearrange("c p f -> p c f")), "lw", 16)
                        I(gpsimd.dma_start(
                            wk_bb[:].rearrange(
                                "p (c f) -> p c f",
                                c=DC)[:, 2 * qg:2 * qg + 2, :],
                            wk[2 * qg:2 * qg + 2].rearrange("c p f -> p c f")),
                          "lw", 16)
                        for o in range(2):
                            W(gpsimd, "ac", b["ac"] + 2 * qg + o + 1)
                            I(gpsimd.dma_start(
                                qdr[(2 * qg + o) * 128:(2 * qg + o + 1) * 128, :],
                                q_stg[o][:]), "st", 16)

                @blk.vector
                def _(vector):
                    def qred(qg2):
                        par2 = 2 * (qg2 % 2)
                        for o in range(2):
                            W(vector, "qB", b["qB"] + 2 * qg2 + o + 1)
                            I(vector.reduce_max(
                                qmax_sb[0:1, 2 * qg2 + o:2 * qg2 + o + 1],
                                pq[par2 + o][0:1, 0:1024], AX.X), "qC")

                    for qg in range(2, 16):
                        qred(qg - 2)
                    qred(14)
                    qred(15)

                @blk.tensor
                def _(tensor):
                    def qss_mms(qg2):
                        par2 = 2 * (qg2 % 2)
                        for o in range(2):
                            W(tensor, "qA", b["qA"] + 2 * qg2 + o + 1)
                            ins2 = None
                            for h in range(2):
                                ins2 = tensor.matmul(
                                    pq[par2 + o][0:1, h * 512:(h + 1) * 512],
                                    c_oc[:], q2s[o][:, h * 512:(h + 1) * 512],
                                    start=True, stop=True)
                            I(ins2, "qB")

                    W(tensor, "cw", b["cw"])
                    for qg in range(16):
                        W(tensor, "lw", b["lw"] + 16 * (2 * qg + 2))
                        if qg >= 2:
                            W(tensor, "ac", b["ac"] + 2 * (qg - 1))
                            qss_mms(qg - 2)
                            W(tensor, "qC", b["qC"] + 2 * (qg - 1))
                        par = 2 * (qg % 2)
                        ins = None
                        for dc in range(DC):
                            for o in range(2):
                                for h in range(2):
                                    ins = tensor.matmul(
                                        pq[par + o][:, h * 512:(h + 1) * 512],
                                        w_bb[qg % 2][:, dc * 256 + o * 128:dc * 256 + (o + 1) * 128],
                                        xo_b[:, dc * 1024 + h * 512:dc * 1024 + (h + 1) * 512],
                                        start=(dc == 0), stop=(dc == DC - 1))
                        I(ins, "mm")
                    W(tensor, "ac", b["ac"] + 32)
                    qss_mms(14)
                    qss_mms(15)

                @blk.scalar
                def _(scalar):
                    for qg in range(16):
                        W(scalar, "mm", b["mm"] + qg + 1)
                        par = 2 * (qg % 2)
                        for o in range(2):
                            W(scalar, "st", b["st"] + 32 * qg)
                            if qg >= 1:
                                W(scalar, "qA", b["qA"] + 2 * (qg - 1) + o + 1)
                            I(scalar.activation(q_stg[o][:], pq[par + o][:],
                                                AF.Copy, scale=SCALE), "ac")
                        for o in range(2):
                            if qg >= 1:
                                W(scalar, "qB", b["qB"] + 2 * (qg - 1) + o + 1)
                            I(scalar.activation(q2s[o][:], q_stg[o][:],
                                                AF.Square), "qA")

        if stop_after == "q":
            _dump_and_stop([(dbg["dq"], qdr)])
            return nc

        with ExitStack() as ph:
            xs = [ph.enter_context(nc.sbuf_tensor(_nm("x2s"), [128, 512], F32))
                  for _ in range(4)]
            xb = [ph.enter_context(nc.sbuf_tensor(_nm("x2b"), [128, 512], BF16))
                  for _ in range(4)]
            k_stg = [ph.enter_context(nc.sbuf_tensor(_nm("ks"), [128, 512], BF16))
                     for _ in range(2)]
            k2stg = [ph.enter_context(nc.sbuf_tensor(_nm("k2s"), [128, 512], BF16))
                     for _ in range(2)]
            kst = ph.enter_context(nc.sbuf_tensor(_nm("kst"), [1, 512], F32))
            kt11 = ph.enter_context(nc.sbuf_tensor(_nm("kt11"), [1, 1], F32))
            pk = [ph.enter_context(nc.psum_tensor(_nm("pk"), [128, 512], F32))
                  for _ in range(8)]
            b = dict(n)
            with nc.Block() as blk:
                @blk.sync
                def _(sync):
                    for ts in range(8):
                        for dc in range(DC):
                            idx = ts * DC + dc
                            if idx >= 4:
                                W(sync, "ac", b["ac"] + idx - 3)
                            I(sync.dma_start(
                                xs[idx % 4][:],
                                xfm[dc][:, ts * 512:(ts + 1) * 512]), "ld", 16)

                @blk.scalar
                def _(scalar):
                    for ts in range(8):
                        for dc in range(DC):
                            idx = ts * DC + dc
                            W(scalar, "ld", b["ld"] + 16 * (idx + 1))
                            if idx >= 4:
                                W(scalar, "mm", b["mm"] + idx - 3)
                            I(scalar.copy(xb[idx % 4][:], xs[idx % 4][:]), "ac")
                        for dkc in range(8):
                            j = ts * 8 + dkc
                            W(scalar, "ev", b["ev"] + j + 1)
                            if j >= 2:
                                W(scalar, "kA", b["kA"] + j - 1)
                            I(scalar.activation(k2stg[j % 2][:], k_stg[j % 2][:],
                                                AF.Square), "kA")

                @blk.tensor
                def _(tensor):
                    W(tensor, "lw", b["lw"])
                    for ts in range(8):
                        for dc in range(DC):
                            idx = ts * DC + dc
                            W(tensor, "ac", b["ac"] + idx + 1)
                            if dc == 0 and ts >= 1:
                                W(tensor, "kB", b["kB"] + 8 * ts)
                            ins = None
                            for dkc in range(8):
                                ins = tensor.matmul(
                                    pk[dkc][:],
                                    wk_bb[:, dc * 1024 + dkc * 128:dc * 1024 + (dkc + 1) * 128],
                                    xb[idx % 4][:],
                                    start=(dc == 0), stop=(dc == DC - 1))
                            I(ins, "mm")
                        for dkc in range(8):
                            j = ts * 8 + dkc
                            W(tensor, "kA", b["kA"] + j + 1)
                            I(tensor.matmul(pk[dkc][0:1, 0:512], c_oc[:],
                                            k2stg[j % 2][:],
                                            start=True, stop=True), "ssc")

                @blk.vector
                def _(vector):
                    for ts in range(8):
                        W(vector, "mm", b["mm"] + (ts + 1) * DC)
                        for dkc in range(8):
                            j = ts * 8 + dkc
                            if j >= 2:
                                W(vector, "st", b["st"] + 16 * (j - 1))
                                W(vector, "kA", b["kA"] + j - 1)
                            I(vector.tensor_copy(k_stg[j % 2][:], pk[dkc][:]), "ev")
                        for dkc in range(8):
                            j = ts * 8 + dkc
                            W(vector, "ssc", b["ssc"] + j + 1)
                            vector.reduce_max(kt11[:], pk[dkc][0:1, 0:512], AX.X)
                            if ts == 0:
                                I(vector.tensor_copy(
                                    kmax_sb[0:1, dkc:dkc + 1], kt11[:]), "kB")
                            else:
                                I(vector.tensor_max(
                                    kmax_sb[0:1, dkc:dkc + 1],
                                    kmax_sb[0:1, dkc:dkc + 1], kt11[:]), "kB")

                @blk.gpsimd
                def _(gpsimd):
                    for ts in range(8):
                        for dkc in range(8):
                            j = ts * 8 + dkc
                            W(gpsimd, "ev", b["ev"] + j + 1)
                            I(gpsimd.dma_start(
                                kdr[dkc * 128:(dkc + 1) * 128, ts * 512:(ts + 1) * 512],
                                k_stg[j % 2][:]), "st", 16)
                            if dkc % 2 == 1:
                                wvc = ts * 4 + dkc // 2
                                I(gpsimd.dma_start(
                                    wv_bb[:, wvc * 1024:(wvc + 1) * 1024],
                                    wv[wvc]), "lw", 16)

        with ExitStack() as ph:
            xs = [ph.enter_context(nc.sbuf_tensor(_nm("x3s"), [128, 512], F32))
                  for _ in range(4)]
            xb = [ph.enter_context(nc.sbuf_tensor(_nm("x3b"), [128, 512], BF16))
                  for _ in range(4)]
            pv = [ph.enter_context(nc.psum_tensor(_nm("pv"), [128, 1024], F32))
                  for _ in range(4)]
            b = dict(n)
            with nc.Block() as blk:
                @blk.sync
                def _(sync):
                    for ts in range(8):
                        for dc in range(DC):
                            idx = ts * DC + dc
                            if idx >= 4:
                                W(sync, "ac", b["ac"] + idx - 3)
                            I(sync.dma_start(
                                xs[idx % 4][:],
                                xfm[dc][:, ts * 512:(ts + 1) * 512]), "ld", 16)

                @blk.scalar
                def _(scalar):
                    for ts in range(8):
                        for dc in range(DC):
                            idx = ts * DC + dc
                            W(scalar, "ld", b["ld"] + 16 * (idx + 1))
                            if idx >= 4:
                                W(scalar, "mm", b["mm"] + idx - 3)
                            I(scalar.copy(xb[idx % 4][:], xs[idx % 4][:]), "ac")

                @blk.tensor
                def _(tensor):
                    W(tensor, "lw", b["lw"])
                    for ts in range(8):
                        for dc in range(DC):
                            idx = ts * DC + dc
                            W(tensor, "ac", b["ac"] + idx + 1)
                            if dc == 0 and ts >= 1:
                                W(tensor, "ev", b["ev"] + 4 * ts)
                            ins = None
                            for tcl in range(4):
                                for h in range(2):
                                    ins = tensor.matmul(
                                        pv[tcl][:, h * 512:(h + 1) * 512],
                                        xb[idx % 4][:, tcl * 128:(tcl + 1) * 128],
                                        wv_bb[:, dc * 1024 + h * 512:dc * 1024 + (h + 1) * 512],
                                        start=(dc == 0), stop=(dc == DC - 1))
                            I(ins, "mm")

                @blk.vector
                def _(vector):
                    for ts in range(8):
                        W(vector, "mm", b["mm"] + (ts + 1) * DC)
                        for tcl in range(4):
                            tc = ts * 4 + tcl
                            I(vector.tensor_copy(
                                v_sb[:, tc * 1024:(tc + 1) * 1024], pv[tcl][:]), "ev")

        if stop_after == "kv":
            _dump_and_stop([(dbg["dq"], qdr), (dbg["dk"], kdr)],
                           [(dbg["dv"], v_sb)])
            return nc

        with ExitStack() as ph:
            K_sb = [ph.enter_context(nc.sbuf_tensor(_nm("Ksb"), [128, S], BF16))
                    for _ in range(2)]
            Q_sb = [ph.enter_context(nc.sbuf_tensor(_nm("Qsb"), [128, T], BF16))
                    for _ in range(2)]
            kmrep = ph.enter_context(nc.sbuf_tensor("kmrep", [1, 32], F32))
            prdrow = ph.enter_context(nc.sbuf_tensor("prdrow", [1, 32], F32))
            sqrow = ph.enter_context(nc.sbuf_tensor("sqrow", [1, 32], F32))
            ngbrow = ph.enter_context(nc.sbuf_tensor("ngbrow", [1, 32], F32))
            negb_all = ph.enter_context(nc.sbuf_tensor("negb_all", [128, 32], F32))
            r_sb = ph.enter_context(nc.sbuf_tensor("r_sb", [1, T], F32))
            R_sb = ph.enter_context(nc.sbuf_tensor("R_sb", [128, T], F32))
            pU = [ph.enter_context(nc.sbuf_tensor(_nm("pU"), [128, T], BF16))
                  for _ in range(2)]
            av_sb = [ph.enter_context(nc.sbuf_tensor(_nm("av"), [128, T], BF16))
                     for _ in range(2)]
            psc = [ph.enter_context(nc.psum_tensor(_nm("psc"), [128, 1024], F32))
                   for _ in range(2)]
            paux = ph.enter_context(nc.psum_tensor("paux", [1, 1024], F32))
            pav = ph.enter_context(nc.psum_tensor("pav", [128, 1024], F32))
            b = dict(n)

            def pieces(kc):
                qlo = 128 * (kc // 4)
                fd = 1024 - qlo
                offs = []
                off = 0
                while off < fd:
                    w_ = min(512, fd - off)
                    offs.append((off, w_))
                    off += w_
                return qlo, fd, offs

            with nc.Block() as blk:
                @blk.sync
                def _(sync):
                    W(sync, "st", n["st"])
                    I(sync.dma_start(
                        msk[:].rearrange("p (c k) -> p c k", c=DC),
                        maskstk[:].rearrange("c p k -> p c k")), "ld", 16)
                    for kvh in range(NKV):
                        if kvh >= 2:
                            W(sync, "ssc", b["ssc"] + 128 * (kvh - 1))
                        I(sync.dma_start(K_sb[kvh % 2][:],
                                         kdr[kvh * 128:(kvh + 1) * 128, :]), "ld", 16)
                        for qr in range(4):
                            it = kvh * 4 + qr
                            if it >= 2:
                                W(sync, "ssc", b["ssc"] + 32 * (it - 1))
                            I(sync.dma_start(Q_sb[it % 2][:],
                                             qdr[it * 128:(it + 1) * 128, :]),
                              "ld", 16)

                @blk.gpsimd
                def _(gpsimd):
                    kv3 = kmrep[:].rearrange("o (k r) -> o k r", r=4)
                    ins = None
                    for j in range(4):
                        ins = gpsimd.tensor_copy(kv3[:, :, j], kmax_sb[:])
                    I(ins, "po")
                    for it in range(32):
                        W(gpsimd, "rC", b["rC"] + it + 1)
                        I(gpsimd.dma_start(audr[it * 128:(it + 1) * 128, :],
                                           av_sb[it % 2][:]), "st", 16)

                @blk.vector
                def _(vector):
                    W(vector, "po", b["po"] + 1)
                    I(vector.tensor_mul(prdrow[:], qmax_sb[:], kmrep[:]), "qA")
                    W(vector, "qB", b["qB"] + 1)
                    I(vector.tensor_scalar_mul(ngbrow[:], sqrow[:], -1.0), "qC")
                    for kvh in range(NKV):
                        for qr in range(4):
                            it = kvh * 4 + qr
                            W(vector, "scn", b["scn"] + 32 * it + 32)
                            W(vector, "mm", b["mm"] + 2 * it + 1)
                            I(vector.reciprocal(r_sb[:], paux[0:1, 0:1024]), "rD")
                            W(vector, "qD", b["qD"] + 3 + 2 * it)
                            if it >= 2:
                                W(vector, "st", b["st"] + 16 * (it - 1))
                            I(vector.tensor_mul(av_sb[it % 2][:], pav[:], R_sb[:]),
                              "rC")

                @blk.scalar
                def _(scalar):
                    W(scalar, "qA", b["qA"] + 1)
                    I(scalar.activation(sqrow[:], prdrow[:], AF.Sqrt), "qB")
                    W(scalar, "mm", b["mm"] + 1)
                    I(scalar.copy(negb_all[:], psc[0][:, 0:32]), "qD")
                    for kvh in range(NKV):
                        for qr in range(4):
                            it = kvh * 4 + qr
                            for kc in range(DC):
                                qlo, fd, offs = pieces(kc)
                                W(scalar, "ssc", b["ssc"] + 32 * it + kc + 1)
                                if kc >= 2:
                                    W(scalar, "scn", b["scn"] + 32 * it + kc - 1)
                                I(scalar.activation(
                                    pU[kc % 2][:, 0:fd], psc[kc % 2][:, 0:fd],
                                    AF.Exp, bias=negb_all[:, it:it + 1]), "spr")
                            W(scalar, "rC", b["rC"] + it)
                            for h in range(2):
                                W(scalar, "mm", b["mm"] + 2 + 2 * it + h)
                                I(scalar.copy(R_sb[:, h * 512:(h + 1) * 512],
                                              psc[1][:, h * 512:(h + 1) * 512]),
                                  "qD")

                @blk.tensor
                def _(tensor):
                    W(tensor, "qC", b["qC"] + 1)
                    I(tensor.matmul(psc[0][:, 0:32], c_1f[:], ngbrow[:],
                                    start=True, stop=True), "mm")
                    for kvh in range(NKV):
                        W(tensor, "ld", b["ld"] + 16 * (5 * kvh + 2))
                        for qr in range(4):
                            it = kvh * 4 + qr
                            W(tensor, "ld", b["ld"] + 16 * (5 * kvh + 3 + qr))
                            def emit_scores(kc):
                                qlo, fd, offs = pieces(kc)
                                pp = psc[kc % 2]
                                W(tensor, "spr", b["spr"] + 32 * it + kc - 1)
                                if kc == 0 and it == 0:
                                    W(tensor, "qD", b["qD"] + 1)
                                if kc == 1:
                                    W(tensor, "qD", b["qD"] + 2 * it + 1)
                                for (off, w_) in offs:
                                    tensor.matmul(
                                        pp[:, off:off + w_],
                                        K_sb[kvh % 2][:, kc * 128:(kc + 1) * 128],
                                        Q_sb[it % 2][:, qlo + off:qlo + off + w_],
                                        start=True, stop=False)
                                I(tensor.matmul(
                                    pp[:, 0:128], c_id[:],
                                    msk[:, kc * 128:(kc + 1) * 128],
                                    start=False, stop=True), "ssc")

                            emit_scores(0)
                            for kc in range(DC):
                                qlo, fd, offs = pieces(kc)
                                if kc + 1 < DC:
                                    emit_scores(kc + 1)
                                W(tensor, "spr", b["spr"] + 32 * it + kc + 1)
                                if kc == 0:
                                    W(tensor, "rD", b["rD"] + it)
                                    W(tensor, "rC", b["rC"] + it)
                                for (off, w_) in offs:
                                    tensor.matmul(
                                        paux[0:1, qlo + off:qlo + off + w_],
                                        c_oc[:], pU[kc % 2][:, off:off + w_],
                                        start=(kc == 0), stop=(kc == DC - 1))
                                ins = None
                                for (off, w_) in offs:
                                    ins = tensor.matmul(
                                        pav[:, qlo + off:qlo + off + w_],
                                        v_sb[:, kc * 1024 + kvh * 128:kc * 1024 + (kvh + 1) * 128],
                                        pU[kc % 2][:, off:off + w_],
                                        start=(kc == 0), stop=(kc == DC - 1))
                                I(ins, "scn")
                            W(tensor, "rD", b["rD"] + it + 1)
                            for h in range(2):
                                I(tensor.matmul(
                                    psc[1][:, h * 512:(h + 1) * 512], c_1f[:],
                                    r_sb[0:1, h * 512:(h + 1) * 512],
                                    start=True, stop=True), "mm")

        if stop_after == "attn":
            _dump_and_stop([(dbg["dau"], audr)])
            return nc


        g_sb_ref = [None]

        def big_proj_phase(wsrc, rhs_sb, res_mode, res_dram, mdst, rdst, lora):
            with ExitStack() as ph:
                w_bb = [ph.enter_context(
                    nc.sbuf_tensor(_nm("Pwb"), [128, DC * 256], BF16))
                    for _ in range(2)]
                x0s = ([ph.enter_context(
                    nc.sbuf_tensor(_nm("Px0"), [128, 1024], F32))
                    for _ in range(2)] if res_mode == "dram" else None)
                xp = [ph.enter_context(
                    nc.sbuf_tensor(_nm("Pxp"), [128, 1024], BF16))
                    for _ in range(2)]
                sq_b = ph.enter_context(nc.sbuf_tensor(_nm("Psq"), [128, 1024], BF16))
                f3 = [ph.enter_context(nc.sbuf_tensor(_nm("Pf"), [1, 1024], F32))
                      for _ in range(3)]
                fin = [f3[0], f3[1], f3[2], f3[1], f3[2], f3[1]]
                pwr = [ph.enter_context(nc.psum_tensor(_nm("Ppw"), [128, 1024], F32))
                       for _ in range(2)]
                pss = ph.enter_context(nc.psum_tensor(_nm("Pss"), [1, 1024], F32))
                psq = ph.enter_context(nc.psum_tensor(_nm("Psq"), [1, 1024], F32))
                b = dict(n)
                nload = 4 if res_mode == "dram" else 2
                with nc.Block() as blk:
                    @blk.sync
                    def _(sync):
                        W(sync, "st", n["st"])
                        if res_mode == "dram":
                            for oc in range(DC):
                                if oc >= 2:
                                    W(sync, "po", b["po"] + oc - 1)
                                I(sync.dma_start(x0s[oc % 2][:], xown[oc]),
                                  "ld", 16)

                    @blk.vector
                    def _(vector):
                        for oc in range(DC):
                            W(vector, "mm", b["mm"] + oc + 1)
                            if res_mode == "dram":
                                W(vector, "ld", b["ld"] + 16 * (oc + 1))
                            if oc >= 2:
                                W(vector, "sA", b["sA"] + oc - 1)
                                W(vector, "st", b["st"] + 16 * (oc - 1))
                            if res_mode == "dram":
                                I(vector.tensor_add(xp[oc % 2][:],
                                                    pwr[oc % 2][:],
                                                    x0s[oc % 2][:]), "po")
                            else:
                                I(vector.tensor_add(
                                    xp[oc % 2][:], pwr[oc % 2][:],
                                    rhs_sb[:, oc * 1024:(oc + 1) * 1024]),
                                  "po")
                        W(vector, "sA", b["sA"] + 32)
                        b["evf"] = n["ev"]
                        I(vector.tensor_scalar_mul(fin[0][:], pss[:],
                                                   1.0 / D), "ev")
                        I(vector.tensor_scalar_mul(fin[1][:], psq[:],
                                                   1.0 / D), "ev")
                        W(vector, "ac", b["ac"] + 33)
                        I(vector.tensor_sub(fin[3][:], fin[1][:], fin[2][:]), "ev")
                        W(vector, "ac", b["ac"] + 34)
                        I(vector.reciprocal(fin[5][:], fin[4][:]), "ev")

                    @blk.tensor
                    def _(tensor):
                        def stats_mms(oc2):
                            W(tensor, "ac", b["ac"] + oc2 + 1)
                            for h in range(2):
                                tensor.matmul(
                                    pss[0:1, h * 512:(h + 1) * 512], c_oc[:],
                                    xp[oc2 % 2][:, h * 512:(h + 1) * 512],
                                    start=(oc2 == 0), stop=(oc2 == 31))
                            ins2 = None
                            for h in range(2):
                                ins2 = tensor.matmul(
                                    psq[0:1, h * 512:(h + 1) * 512], c_oc[:],
                                    sq_b[:, h * 512:(h + 1) * 512],
                                    start=(oc2 == 0), stop=(oc2 == 31))
                            I(ins2, "sA")

                        for oc in range(DC):
                            og, o = oc // 2, oc % 2
                            W(tensor, "lw", b["lw"] + 16 * (og + 1))
                            if oc >= 2:
                                W(tensor, "po", b["po"] + oc - 1)
                            pw = pwr[oc % 2]
                            ins = None
                            for dc in range(DC):
                                for h in range(2):
                                    ins = tensor.matmul(
                                        pw[:, h * 512:(h + 1) * 512],
                                        w_bb[og % 2][:, dc * 256 + o * 128:dc * 256 + (o + 1) * 128],
                                        rhs_sb[:, dc * 1024 + h * 512:dc * 1024 + (h + 1) * 512],
                                        start=(dc == 0),
                                        stop=(dc == DC - 1 and not lora))
                            if lora:
                                for h in range(2):
                                    tensor.matmul(
                                        pw[:, h * 512:(h + 1) * 512],
                                        lb_b[:, oc * 128:(oc + 1) * 128],
                                        g_sb_ref[0][:, h * 512:(h + 1) * 512],
                                        start=False, stop=False)
                                for h in range(2):
                                    ins = tensor.matmul(
                                        pw[:, h * 512:(h + 1) * 512],
                                        bb_b[0:1, oc * 128:(oc + 1) * 128],
                                        c_or[:],
                                        start=False, stop=True)
                            I(ins, "mm")
                            if oc >= 2:
                                stats_mms(oc - 2)
                        stats_mms(30)
                        stats_mms(31)
                        W(tensor, "po", b["po"] + 32)
                        W(tensor, "ev", b["evf"] + 4)
                        for h in range(2):
                            tensor.matmul(pwr[0][:, h * 512:(h + 1) * 512],
                                          c_1f[:],
                                          fin[0][0:1, h * 512:(h + 1) * 512],
                                          start=True, stop=True)
                        ins = None
                        for h in range(2):
                            ins = tensor.matmul(pwr[1][:, h * 512:(h + 1) * 512],
                                                c_1f[:],
                                                fin[5][0:1, h * 512:(h + 1) * 512],
                                                start=True, stop=True)
                        I(ins, "mm")

                    @blk.scalar
                    def _(scalar):
                        for oc in range(DC):
                            W(scalar, "po", b["po"] + oc + 1)
                            W(scalar, "sA", b["sA"] + oc)
                            I(scalar.activation(sq_b[:], xp[oc % 2][:],
                                                AF.Square), "ac")
                        W(scalar, "ev", b["evf"] + 1)
                        I(scalar.activation(fin[2][:], fin[0][:], AF.Square), "ac")
                        W(scalar, "ev", b["evf"] + 3)
                        I(scalar.activation(fin[4][:], fin[3][:], AF.Sqrt,
                                            bias=eps11[:]), "ac")
                        W(scalar, "mm", b["mm"] + 33)
                        I(scalar.copy(mdst[:], pwr[0][:]), "ac")
                        I(scalar.copy(rdst[:], pwr[1][:]), "ac")

                    @blk.gpsimd
                    def _(gpsimd):
                        for og in range(2):
                            I(gpsimd.dma_start(
                                w_bb[og % 2][:].rearrange("p (c f) -> p c f",
                                                          c=DC),
                                wsrc[og].rearrange("c p f -> p c f")), "lw", 16)
                        for oc in range(DC):
                            W(gpsimd, "po", b["po"] + oc + 1)
                            I(gpsimd.dma_start(res_dram[oc], xp[oc % 2][:]),
                              "st", 16)
                            if oc % 2 == 1 and oc // 2 + 2 < 16:
                                og2 = oc // 2 + 2
                                W(gpsimd, "mm", b["mm"] + 2 * og2 - 2)
                                I(gpsimd.dma_start(
                                    w_bb[og2 % 2][:].rearrange(
                                        "p (c f) -> p c f", c=DC),
                                    wsrc[og2].rearrange("c p f -> p c f")),
                                  "lw", 16)

        with ExitStack() as ph6:
            au_sb = bigA
            b6 = dict(n)
            with nc.Block() as blk:
                @blk.sync
                def _(sync):
                    W(sync, "st", n["st"])
                    I(sync.dma_start(
                        au_sb[:].rearrange("p (c t) -> p c t", c=DC),
                        audr[:].rearrange("(c p) t -> p c t", p=128)), "ld", 16)

                @blk.tensor
                def _(tensor):
                    W(tensor, "ld", b6["ld"] + 16)
            big_proj_phase(wo, au_sb, "dram", x1dr, m1, r1, lora=False)

        ph78 = top.enter_context(ExitStack())
        m2 = ph78.enter_context(nc.sbuf_tensor("m2", [128, T], F32))
        r2 = ph78.enter_context(nc.sbuf_tensor("r2", [128, T], F32))
        x1_sb = bigB
        la_b = bigA[:, 0:DC * 128]
        lb_b = bigA[:, DC * 128:DC * 256]
        with ExitStack() as ph7:
            with ExitStack() as ph:
                xps = [ph.enter_context(nc.sbuf_tensor(_nm("l1a"), [128, 1024],
                                                       BF16)) for _ in range(2)]
                tmp = [ph.enter_context(nc.sbuf_tensor(_nm("l1t"), [128, 1024],
                                                       F32)) for _ in range(2)]
                b = dict(n)
                with nc.Block() as blk:
                    @blk.sync
                    def _(sync):
                        W(sync, "st", n["st"])
                        for oc in range(DC):
                            if oc >= 2:
                                W(sync, "ev", b["ev"] + oc - 1)
                            I(sync.dma_start(xps[oc % 2][:], x1dr[oc]), "ld", 16)

                    @blk.vector
                    def _(vector):
                        for oc in range(DC):
                            W(vector, "ld", b["ld"] + 16 * (oc + 1))
                            if oc >= 2:
                                W(vector, "po", b["po"] + oc - 1)
                            I(vector.tensor_sub(tmp[oc % 2][:], xps[oc % 2][:],
                                                m1[:]), "ev")

                    @blk.gpsimd
                    def _(gpsimd):
                        I(gpsimd.dma_start(
                            la_b.rearrange("p (c f) -> p c f", c=DC),
                            la[:].rearrange("c p f -> p c f")), "lw", 16)
                        I(gpsimd.dma_start(
                            lb_b.rearrange("p (c f) -> p c f", c=DC),
                            lb[:].rearrange("c f p -> f c p")), "lw", 16)
                        for oc in range(DC):
                            W(gpsimd, "ev", b["ev"] + oc + 1)
                            I(gpsimd.tensor_mul(
                                x1_sb[:, oc * 1024:(oc + 1) * 1024],
                                tmp[oc % 2][:], r1[:]), "po")

            if stop_after == "x1":
                _dump_and_stop([], [(dbg["dx1"], x1_sb)])
                return nc

            g_sb = ph7.enter_context(nc.sbuf_tensor("g_sb", [128, T], BF16))
            g_sb_ref[0] = g_sb
            with ExitStack() as ph:
                z_sb = ph.enter_context(nc.sbuf_tensor("z_sb", [128, 64], F32))
                zs = ph.enter_context(nc.sbuf_tensor("zs", [128, 64], F32))
                cum = ph.enter_context(nc.sbuf_tensor("cum", [128, 64], F32))
                tmp6 = ph.enter_context(nc.sbuf_tensor("tmp6", [128, 64], F32))
                ind = ph.enter_context(nc.sbuf_tensor("ind", [128, 64], F32))
                kz8 = ph.enter_context(nc.sbuf_tensor("kz8", [128, 8], F32))
                ts8 = ph.enter_context(nc.sbuf_tensor("ts8", [128, 8], F32))
                tau8 = ph.enter_context(nc.sbuf_tensor("tau8", [128, 8], F32))
                kzr = ph.enter_context(nc.sbuf_tensor("kzr", [128, 8], F32))
                trep = ph.enter_context(nc.sbuf_tensor("trep", [128, 64], F32))
                w_f = ph.enter_context(nc.sbuf_tensor("w_f", [128, 64], F32))
                w_b2 = ph.enter_context(nc.sbuf_tensor("w_b2", [128, 64], BF16))
                weT = ph.enter_context(nc.sbuf_tensor("weT", [E, T], BF16))
                wer = ph.enter_context(nc.sbuf_tensor("wer", [128, T], BF16))
                plg = ph.enter_context(nc.psum_tensor("plg", [128, E], F32))
                ptr8 = ph.enter_context(nc.psum_tensor("ptr8", [E, 128], BF16))
                pwer = ph.enter_context(nc.psum_tensor("pwer", [128, 1024], F32))
                phh = ph.enter_context(nc.psum_tensor("phh", [128, 1024], F32))

                b = dict(n)
                with nc.Block() as blk:
                    @blk.tensor
                    def _(tensor):
                        for tc in range(8):
                            if tc >= 1:
                                W(tensor, "ac", b["ac"] + tc)
                            for dc in range(DC):
                                tensor.matmul(
                                    plg[:],
                                    x1_sb[:, dc * 1024 + tc * 128:dc * 1024 + (tc + 1) * 128],
                                    wr_b[:, dc * E:(dc + 1) * E],
                                    start=(dc == 0), stop=False)
                            I(tensor.matmul(plg[:], c_1b[:], br_b[:],
                                            start=False, stop=True), "mm")

                    @blk.scalar
                    def _(scalar):
                        for tc in range(8):
                            W(scalar, "mm", b["mm"] + tc + 1)
                            I(scalar.copy(z_sb[:, tc * 8:(tc + 1) * 8], plg[:]),
                              "ac")

                def v3(t_):
                    return t_[:].rearrange("p (g e) -> p g e", e=8)

                def v5(t_):
                    return t_[:].rearrange("p (g i j k) -> p g i j k",
                                           i=2, j=2, k=2)

                def sva(t_, off):
                    return t_[:].rearrange("p (g a e) -> p g a e",
                                           a=2, e=4)[:, :, :, off:off + 1]

                def s6v(t_, off):
                    return v3(t_)[:, :, off:off + 6:2]

                ops = []

                def add(fn, dve_only=True):
                    ops.append((fn, False))

                def addc(fn):
                    ops.append((fn, True))

                addc(lambda e: e.tensor_copy(zs[:], z_sb[:]))
                for (va, vb) in [
                    (lambda t: v5(t)[:, :, :, :, 0], lambda t: v5(t)[:, :, :, :, 1]),
                    (lambda t: v5(t)[:, :, :, 0, :], lambda t: v5(t)[:, :, :, 1, :]),
                    (lambda t: sva(t, 1), lambda t: sva(t, 2)),
                    (lambda t: v5(t)[:, :, 0, :, :], lambda t: v5(t)[:, :, 1, :, :]),
                    (lambda t: v3(t)[:, :, 2:4], lambda t: v3(t)[:, :, 4:6]),
                    (lambda t: s6v(t, 1), lambda t: s6v(t, 2)),
                ]:
                    add(lambda e, va=va, vb=vb: e.tensor_tensor(
                        va(tmp6), va(zs), vb(zs), ALU.max))
                    add(lambda e, va=va, vb=vb: e.tensor_tensor(
                        vb(tmp6), va(zs), vb(zs), ALU.min))
                    addc(lambda e, va=va, vb=vb: e.tensor_copy(va(zs), va(tmp6)))
                    addc(lambda e, va=va, vb=vb: e.tensor_copy(vb(zs), vb(tmp6)))
                addc(lambda e: e.tensor_copy(cum[:], zs[:]))
                for d_ in (1, 2, 4):
                    add(lambda e, d_=d_: e.tensor_tensor(
                        v3(tmp6)[:, :, d_:8], v3(cum)[:, :, d_:8],
                        v3(cum)[:, :, 0:8 - d_], ALU.add))
                    addc(lambda e, d_=d_: e.tensor_copy(
                        v3(cum)[:, :, d_:8], v3(tmp6)[:, :, d_:8]))
                add(lambda e: e.tensor_mul(tmp6[:], zs[:], c_kv[:]))
                add(lambda e: e.tensor_sub(tmp6[:], tmp6[:], cum[:]))
                add(lambda e: e.tensor_single_scalar(ind[:], tmp6[:], -1.0,
                                                     ALU.is_gt))
                add(lambda e: e.tensor_reduce(kz8[:], v3(ind), AX.X, ALU.add))
                addc(lambda e: e.tensor_copy(tmp6[:], ind[:]))
                add(lambda e: e.tensor_tensor(
                    v3(tmp6)[:, :, 0:7], v3(ind)[:, :, 0:7], v3(ind)[:, :, 1:8],
                    ALU.subtract))
                add(lambda e: e.tensor_mul(tmp6[:], tmp6[:], cum[:]))
                add(lambda e: e.tensor_reduce(ts8[:], v3(tmp6), AX.X, ALU.add))
                add(lambda e: e.tensor_single_scalar(ts8[:], ts8[:], 1.0,
                                                     ALU.subtract))
                add(lambda e: e.reciprocal(kzr[:], kz8[:]), dve_only=True)
                add(lambda e: e.tensor_mul(tau8[:], ts8[:], kzr[:]))
                addc(lambda e: e.tensor_copy(v3(trep)[:, :, 0], tau8[:]))
                addc(lambda e: e.tensor_copy(v3(trep)[:, :, 1], tau8[:]))
                addc(lambda e: e.tensor_copy(v3(trep)[:, :, 2:4],
                                            v3(trep)[:, :, 0:2]))
                addc(lambda e: e.tensor_copy(v3(trep)[:, :, 4:8],
                                            v3(trep)[:, :, 0:4]))
                add(lambda e: e.tensor_sub(w_f[:], z_sb[:], trep[:]))
                add(lambda e: e.tensor_scalar_max(w_f[:], w_f[:], 0.0))

                sched = []
                last = 1
                for (fn, pool_ok) in ops:
                    if pool_ok and last == 0:
                        sched.append((1, fn))
                    else:
                        if last == 0:
                            sched.append((1, None))
                        sched.append((0, fn))
                    last = sched[-1][0]
                bnc = ph.enter_context(nc.sbuf_tensor("bnc", [1, 4], F32))
                b = dict(n)
                nv = sum(1 for e_, _ in sched if e_ == 0)
                np_ = sum(1 for e_, _ in sched if e_ == 1)
                with nc.Block() as blk:
                    @blk.vector
                    def _(vector):
                        cv = cp = 0
                        for (e_, fn) in sched:
                            if e_ == 0:
                                W(vector, "po", b["po"] + cp)
                                fn_ = fn if fn else (
                                    lambda e: e.tensor_copy(bnc[0:1, 0:2],
                                                            bnc[0:1, 2:4]))
                                I(fn_(vector), "ev")
                                cv += 1
                            else:
                                cp += 1

                    @blk.gpsimd
                    def _(gpsimd):
                        cv = cp = 0
                        for (e_, fn) in sched:
                            if e_ == 1:
                                W(gpsimd, "ev", b["ev"] + cv)
                                fn_ = fn if fn else (
                                    lambda e: e.tensor_copy(bnc[0:1, 0:2],
                                                            bnc[0:1, 2:4]))
                                I(fn_(gpsimd), "po")
                                cp += 1
                            else:
                                cv += 1

                if stop_after == "w":
                    _dump_and_stop([], [(dbg["dz"], z_sb), (dbg["dw"], w_f)])
                    return nc

                b = dict(n)
                with nc.Block() as blk:
                    @blk.scalar
                    def _(scalar):
                        I(scalar.copy(w_b2[:], w_f[:]), "ac")
                        for tc in range(8):
                            W(scalar, "mm", b["mm"] + tc + 1)
                            I(scalar.copy(weT[0:E, tc * 128:(tc + 1) * 128],
                                          ptr8[:]), "ac")
                        W(scalar, "mm", b["mm"] + 9)
                        I(scalar.copy(wer[:], pwer[:]), "ac")

                    @blk.tensor
                    def _(tensor):
                        W(tensor, "lw", n["lw"])
                        for tc in range(8):
                            W(tensor, "ac", b["ac"] + 1 + tc)
                            I(tensor.transpose(ptr8[:],
                                               w_b2[:, tc * 8:(tc + 1) * 8],
                                               c_id[:]), "mm")
                        W(tensor, "ac", b["ac"] + 9)
                        ins = None
                        for h in range(2):
                            ins = tensor.matmul(
                                pwer[:, h * 512:(h + 1) * 512], c_e8[:],
                                weT[0:E, h * 512:(h + 1) * 512],
                                start=True, stop=True)
                        I(ins, "mm")
                        W(tensor, "ac", b["ac"] + 10)
                        ins = None
                        for dc in range(DC):
                            for h in range(2):
                                ins = tensor.matmul(
                                    phh[:, h * 512:(h + 1) * 512],
                                    la_b[:, dc * 128:(dc + 1) * 128],
                                    x1_sb[:, dc * 1024 + h * 512:dc * 1024 + (h + 1) * 512],
                                    start=(dc == 0), stop=(dc == DC - 1))
                        I(ins, "mm")

                    @blk.vector
                    def _(vector):
                        W(vector, "mm", b["mm"] + 10)
                        I(vector.tensor_mul(g_sb[:], phh[:], wer[:]), "ev")

            if stop_after == "g":
                _dump_and_stop([], [(dbg["dz"], z_sb), (dbg["dw"], w_f),
                                    (dbg["dg"], g_sb)])
                return nc

            big_proj_phase(wb, x1_sb, "sbuf", x2dr, m2, r2, lora=True)

        if stop_after == "x2":
            _dump_and_stop([(dbg["dx2"], x2dr)],
                           [(dbg["dm2"], m2), (dbg["dr2"], r2)])
            return nc

        with ExitStack() as ph:
            xps = [ph.enter_context(nc.sbuf_tensor(_nm("l2a"), [128, 1024], BF16))
                   for _ in range(2)]
            tmp = [ph.enter_context(nc.sbuf_tensor(_nm("l2t"), [128, 1024], F32))
                   for _ in range(2)]
            oc_f = [ph.enter_context(nc.sbuf_tensor(_nm("l2o"), [128, 1024], F32))
                    for _ in range(2)]
            b = dict(n)
            with nc.Block() as blk:
                @blk.sync
                def _(sync):
                    W(sync, "st", n["st"])
                    for oc in range(DC):
                        if oc >= 2:
                            W(sync, "ev", b["ev"] + oc - 1)
                        I(sync.dma_start(xps[oc % 2][:], x2dr[oc]), "ld", 16)
                        if oc >= 2:
                            W(sync, "po", b["po"] + oc - 1)
                            I(sync.dma_start(
                                out[(oc - 2) * 128:(oc - 1) * 128, :],
                                oc_f[oc % 2][:]), "st", 16)
                    for oc in range(DC - 2, DC):
                        W(sync, "po", b["po"] + oc + 1)
                        I(sync.dma_start(out[oc * 128:(oc + 1) * 128, :],
                                         oc_f[oc % 2][:]), "st", 16)
                    W(sync, "st", b["st"] + 16 * DC)

                @blk.vector
                def _(vector):
                    for oc in range(DC):
                        W(vector, "ld", b["ld"] + 16 * (oc + 1))
                        if oc >= 2:
                            W(vector, "po", b["po"] + oc - 1)
                        I(vector.tensor_sub(tmp[oc % 2][:], xps[oc % 2][:], m2[:]),
                          "ev")

                @blk.gpsimd
                def _(gpsimd):
                    for oc in range(DC):
                        W(gpsimd, "ev", b["ev"] + oc + 1)
                        if oc >= 2:
                            W(gpsimd, "st", b["st"] + 16 * (oc - 1))
                        I(gpsimd.tensor_mul(oc_f[oc % 2][:], tmp[oc % 2][:],
                                            r2[:]), "po")

    return nc


def kernel(**inputs):
    in_maps, own = _stage_inputs(inputs)
    nc = build_program()
    res = run_bass_kernel_spmd(nc, in_maps, core_ids=list(range(NCORE)))
    outp = np.zeros((B, S, D), dtype=np.float32)
    for c in range(NCORE):
        b_, rows = own[c]
        o = res.results[c]["out"]
        outp[b_][rows] = o.T
    return outp
